# version 10
# baseline (speedup 1.0000x reference)
import numpy as np
import concourse.bass as bass
import concourse.mybir as mybir
from concourse.bass_utils import run_bass_kernel_spmd
from concourse.alu_op_type import AluOpType

F32 = mybir.dt.float32
BF16 = mybir.dt.bfloat16
AF = mybir.ActivationFunctionType
ALU = AluOpType

D = 2048; L = 2048; R = 2048; DH = 512; E = 16; H = 1024; HS = 4096
XS = 2050
TD = 16; NJ = 16; NTT = 16
DEBUG = False


SIM_NO_RACE = False
PHASES = 99


class _Done(Exception):
    pass


def _replay(block, streams):
    @block.sync
    def _(eng):
        for f in streams["sync"]:
            f(eng)

    @block.gpsimd
    def _(eng):
        for f in streams["gpsimd"]:
            f(eng)

    @block.tensor
    def _(eng):
        for f in streams["tensor"]:
            f(eng)

    @block.vector
    def _(eng):
        for f in streams["vector"]:
            f(eng)

    @block.scalar
    def _(eng):
        for f in streams["scalar"]:
            f(eng)


def build_nc():
    nc = bass.Bass("TRN2", target_bir_lowering=False,
                   detect_race_conditions=not SIM_NO_RACE)

    def din(name, shape):
        return nc.dram_tensor(name, shape, F32, kind="ExternalInput").ap()
    x_d = din("x", [L, D])
    Wu_d = din("Wu", [R, D]); bu_d = din("bu", [R])
    Wv_d = din("Wv", [R, D]); bv_d = din("bv", [R])
    Wg1_d = din("Wg1", [DH, D]); bg1_d = din("bg1", [DH])
    Wg2_d = din("Wg2", [R, DH]); bg2_d = din("bg2", [R])
    convw_d = din("conv_w", [D, 3]); convb_d = din("conv_b", [D])
    wr_d = din("w_router", [D, E])
    we1_d = din("we1", [E * D, H]); we3_d = din("we3", [E * D, H]); we2_d = din("we2", [E * H, D])
    ws1_d = din("ws1", [D, HS]); ws3_d = din("ws3", [D, HS]); ws2_d = din("ws2", [HS, D])
    out_d = nc.dram_tensor("out", [L, D], F32, kind="ExternalOutput").ap()
    wuT_scr = nc.dram_tensor("wuT_scr", [R, D], BF16, kind="Internal").ap()
    attnx_scr = nc.dram_tensor("attnx_scr", [L, D], F32, kind="Internal").ap()
    convT_scr = nc.dram_tensor("convT_scr", [L, D], F32, kind="Internal").ap()
    dbg = {}
    if DEBUG:
        for nm, shape, dt in [("d_xn", [128, TD * XS], BF16), ("d_h", [128, 4 * 2048], BF16),
                              ("d_go", [128, NJ * 2048], BF16), ("d_cw", [16, 2048], BF16),
                              ("d_acc", [128, 8 * 2048], F32), ("d_ug", [128, 2048], F32),
                              ("d_cum", [128, 2048], F32), ("d_ax", [L, D], F32)]:
            dbg[nm] = nc.dram_tensor(nm, shape, dt, kind="ExternalOutput").ap()

    BASE = 16512
    def at(name, shape, dtype, off):
        return nc.alloc_sbuf_tensor_at(name, shape, dtype, offset=BASE + off)

    xn_sb = at("xn_sb", [128, TD * XS], BF16, 0)
    ident_bf = at("ident_bf", [128, 128], BF16, 65600)
    ident_f128 = at("ident_f128", [128, 128], F32, 65856)
    onesrow = at("onesrow", [1, 128], BF16, 66368)
    small = at("small", [128, 112], F32, 66624)
    convw_sb = at("convw_sb", [128, 48], F32, 67072)
    cwT_sb = at("cwT_sb", [16, 2048], BF16, 67264)
    cwrow = at("cwrow", [1, 2048], BF16, 71360)
    RA = 75456
    TOP = RA + 135168
    rtr_bf = at("rtr_bf", [128, 256], BF16, TOP)
    lg = at("lg", [128, 16], F32, TOP + 512)
    pp = at("pp", [128, 16], F32, TOP + 576)
    rr = at("rr", [128, 16], F32, TOP + 640)
    sel = at("sel", [128, 16], F32, TOP + 704)
    cwt_t = at("cwt_t", [128, 16], BF16, TOP + 768)
    cw02 = at("cw02", [128, 1024], BF16, TOP)
    go_sb = at("go_sb", [128, NJ * 2048], BF16, RA)
    h_sb = at("h_sb", [128, 4 * 2048], BF16, RA + 65536)
    wuT_b = [at(f"wuT{k}", [128, 2048], BF16, RA + 81920 + 4096 * k) for k in range(2)]
    wvT_b = [at(f"wvT{k}", [128, 2048], BF16, RA + 90112 + 4096 * k) for k in range(2)]
    wg2T_b = [at(f"wg2T{k}", [128, 512], BF16, RA + 98304 + 1024 * k) for k in range(2)]
    raw_b = [at(f"raw{k}", [128, 2048], BF16, RA + 100352 + 4096 * k) for k in range(2)]
    wg2raw_b = [at(f"wg2raw{k}", [128, 512], BF16, RA + 108544 + 1024 * k) for k in range(2)]
    u_g = at("u_g", [128, 2048], F32, RA + 110592)
    cum = at("cum", [128, 2048], F32, RA + 118784)
    gates = at("gates", [128, 2048], BF16, RA + 126976)
    v_g = at("v_g", [128, 2048], BF16, RA + 131072)
    xstage = [at("xst0", [128, 2048], F32, RA + 110592),
              at("xst1", [128, 2048], F32, RA + 118784)]
    cen = at("cen", [128, 2048], F32, RA + 126976)
    xntok = at("xntok", [128, 2048], BF16, RA + 81920)
    ltT = at("ltT", [16, 2048], F32, RA + 90112)
    st1 = at("st1", [128, 1], F32, 66624 + 320)
    st2 = at("st2", [128, 1], F32, 66624 + 352)
    st3 = at("st3", [128, 1], F32, 66624 + 384)
    eps_c = at("eps_c", [128, 1], F32, 66624 + 416)
    diagb = [at(f"diag{k}", [128, 128], BF16, RA + 118784 + 256 * k) for k in range(6)]
    cbrow = at("cbrow", [1, 2048], BF16, RA + 110592)
    cstage = at("cstage", [128, 2048], F32, RA + 126976)
    p4wuT = [at("p4wuT0", [128, 8192], BF16, RA + 65536),
             at("p4wuT1", [128, 8192], BF16, RA + 81920)]
    p4ax = [at("p4ax0", [128, 2048], F32, RA + 126976),
            at("p4ax1", [128, 2048], F32, RA + 100352),
            at("p4ax2", [128, 2048], F32, RA + 110592),
            at("p4ax3", [128, 2048], F32, RA + 118784)]
    acc = at("acc", [128, 8 * 2048], F32, RA)
    g_e = at("g_e", [128, 8 * 1024], BF16, RA + 65536)
    g1pair = at("g1pair", [128, 2048], F32, RA + 81920)
    tmp512 = at("tmp512", [128, 512], F32, RA + 90112)
    we1buf = [at(f"we1b{k}", [128, 4096], BF16, RA + 92160 + 8192 * k) for k in range(2)]
    we2buf = [at(f"we2b{k}", [128, 4096], BF16, RA + 108544 + 8192 * k) for k in range(2)]
    cwbb = [at(f"cwb{k}", [128, 1024], BF16, RA + 124928 + 2048 * k) for k in range(2)]
    BU, BV, BG2, CB, BG1 = 0, 16, 32, 48, 64

    import contextlib
    stk = contextlib.ExitStack()
    pb = [stk.enter_context(nc.psum_tensor(f"pb{k}", [128, 512], F32)) for k in range(6)]
    pT = [stk.enter_context(nc.psum_tensor(f"pT{k}", [128, 1024], BF16)) for k in range(2)]
    semGW = stk.enter_context(nc.semaphore("semGW"))
    semSX = stk.enter_context(nc.semaphore("semSX"))
    semPE = stk.enter_context(nc.semaphore("semPE"))
    semDV = stk.enter_context(nc.semaphore("semDV"))
    semAC = stk.enter_context(nc.semaphore("semAC"))
    semGP = stk.enter_context(nc.semaphore("semGP"))
    semAX = stk.enter_context(nc.semaphore("semAX"))
    block = stk.enter_context(nc.Block())

    SEMS = {"gw": semGW, "sx": semSX, "pe": semPE, "dv": semDV, "ac": semAC, "gp": semGP, "ax": semAX}
    DMA_SEMS = {"gw", "sx", "ax"}
    counts = {k: 0 for k in SEMS}
    ENGS = ["sync", "gpsimd", "tensor", "vector", "scalar"]
    waited = {e: {} for e in ENGS}
    streams = {e: [] for e in ENGS}

    def emit(engn, fn):
        streams[engn].append(fn)

    def wait(engn, **need):
        todo = [(k, v) for k, v in need.items() if v and v > waited[engn].get(k, 0)]
        if not todo:
            return
        for k, v in todo:
            waited[engn][k] = v
        def f(eng, todo=tuple(todo)):
            for k, v in todo:
                eng.wait_ge(SEMS[k], 16 * v if k in DMA_SEMS else v)
        emit(engn, f)

    def step(engn, semkey, buildfn):
        counts[semkey] += 1
        amt = 16 if semkey in DMA_SEMS else 1
        def f(eng, buildfn=buildfn, amt=amt, semkey=semkey):
            buildfn(eng).then_inc(SEMS[semkey], amt)
        emit(engn, f)
        return counts[semkey]

    def raw(engn, buildfn):
        emit(engn, lambda eng, b=buildfn: b(eng))

    def snap():
        return dict(counts)

    def merge(*deps):
        out = {}
        for d in deps:
            if not d:
                continue
            for k, v in d.items():
                if v:
                    out[k] = max(out.get(k, 0), v)
        return out

    pT_free = [None, None]
    evac_eng = ["vector", "scalar"]

    def transp_batch(src_buf, dst_off_fn, nt, pre_evac=None, batch_ap_fn=None):
            ev_all = {}
            first = [True, True]
            for b0 in range(0, nt, 8):
                kb = (b0 // 8) % 2
                bank = pT[kb]
                wait("tensor", **merge(pT_free[kb]))
                n8 = min(8, nt - b0)
                pe = None
                for s in range(n8):
                    i = b0 + s
                    pe = step("tensor", "pe", lambda e, bank=bank, s=s, i=i, src=src_buf: e.transpose(
                        bank[:, s * 128:(s + 1) * 128], src[:, i * 128:(i + 1) * 128], ident_bf[:]))
                en = evac_eng[kb]
                key = "dv" if en == "vector" else "ac"
                w = {"pe": pe}
                if pre_evac and first[kb]:
                    w = merge(w, pre_evac)
                first[kb] = False
                wait(en, **w)
                if batch_ap_fn is not None:
                    out_ap = batch_ap_fn(b0, n8)
                    if en == "vector":
                        c = step(en, key, lambda e, bank=bank, n8=n8, out_ap=out_ap: e.tensor_copy(
                            out_ap, bank[:, 0:n8 * 128]))
                    else:
                        c = step(en, key, lambda e, bank=bank, n8=n8, out_ap=out_ap: e.copy(
                            out_ap, bank[:, 0:n8 * 128]))
                else:
                    c = None
                    for s in range(n8):
                        i = b0 + s
                        dst, off = dst_off_fn(i)
                        if en == "vector":
                            c = step(en, key, lambda e, bank=bank, s=s, dst=dst, off=off: e.tensor_copy(
                                dst[:, off:off + 128], bank[:, s * 128:(s + 1) * 128]))
                        else:
                            c = step(en, key, lambda e, bank=bank, s=s, dst=dst, off=off: e.copy(
                                dst[:, off:off + 128], bank[:, s * 128:(s + 1) * 128]))
                evh = {key: c}
                pT_free[kb] = evh
                ev_all = merge(ev_all, evh)
            return ev_all

    raw("gpsimd", lambda e: e.memset(xn_sb[:], 0.0))
    g0 = step("gpsimd", "gp", lambda e: e.memset(ident_bf[:], 0.0))
    wait("gpsimd", gp=g0)
    raw("gpsimd", lambda e: e.affine_select(out=ident_bf[:], in_=ident_bf[:],
        compare_op=ALU.not_equal, fill=1.0, base=0, pattern=[[-1, 128]], channel_multiplier=1))
    g0 = step("gpsimd", "gp", lambda e: e.memset(ident_f128[:], 0.0))
    wait("gpsimd", gp=g0)
    raw("gpsimd", lambda e: e.affine_select(out=ident_f128[:], in_=ident_f128[:],
        compare_op=ALU.not_equal, fill=1.0, base=0, pattern=[[-1, 128]], channel_multiplier=1))
    raw("gpsimd", lambda e: e.memset(onesrow[:], 1.0))
    gp_const = step("gpsimd", "gp", lambda e: e.memset(eps_c[:], 1e-5))

    bu_r = bu_d.rearrange("(a p) -> a p", p=128)
    bv_r = bv_d.rearrange("(a p) -> a p", p=128)
    bg2_r = bg2_d.rearrange("(a p) -> a p", p=128)
    cb_r = convb_d.rearrange("(a p) -> a p", p=128)
    bg1_r = bg1_d.rearrange("(a p) -> a p", p=128)
    for j in range(16):
        step("sync", "sx", lambda e, j=j: e.dma_start(out=small[:, BU + j:BU + j + 1], in_=bu_r[j:j + 1, :]))
        step("sync", "sx", lambda e, j=j: e.dma_start(out=small[:, BV + j:BV + j + 1], in_=bv_r[j:j + 1, :]))
        step("sync", "sx", lambda e, j=j: e.dma_start(out=small[:, BG2 + j:BG2 + j + 1], in_=bg2_r[j:j + 1, :]))
        step("sync", "sx", lambda e, j=j: e.dma_start(out=small[:, CB + j:CB + j + 1], in_=cb_r[j:j + 1, :]))
    for m in range(4):
        step("sync", "sx", lambda e, m=m: e.dma_start(out=small[:, BG1 + m:BG1 + m + 1], in_=bg1_r[m:m + 1, :]))
    for i in range(15):
        step("sync", "sx", lambda e, i=i: e.dma_start(out=convw_sb[:, i * 3:(i + 1) * 3], in_=convw_d[i * 128:(i + 1) * 128, :]))
    sx_small = step("sync", "sx", lambda e: e.dma_start(out=convw_sb[:, 45:48], in_=convw_d[15 * 128:, :]))
    for i in range(15):
        step("gpsimd", "gw", lambda e, i=i: e.dma_start(out=rtr_bf[:, i * 16:(i + 1) * 16], in_=wr_d[i * 128:(i + 1) * 128, :]))
    gw_rtr = step("gpsimd", "gw", lambda e: e.dma_start(out=rtr_bf[:, 240:256], in_=wr_d[15 * 128:, :]))

    try:
        wait("tensor", gp=gp_const)
        wg1T = [at(f"wg1T{m}", [128, 2048], BF16, RA + m * 4096) for m in range(4)]
        raw_pe_free = [0, 0]
        wg1_ready = [None] * 4
        for m in range(4):
            k = m % 2
            wait("gpsimd", pe=raw_pe_free[k])
            gw = step("gpsimd", "gw", lambda e, m=m, k=k: e.dma_start(out=raw_b[k][:], in_=Wg1_d[m * 128:(m + 1) * 128, :]))
            wait("tensor", gw=gw)
            wg1_ready[m] = transp_batch(raw_b[k], None, 16,
                                        batch_ap_fn=lambda b0, n8, m=m: wg1T[m][:, b0 * 128:(b0 + n8) * 128])
            raw_pe_free[k] = counts["pe"]
        xst_free = [None, None]
        xntok_free = None
        p0_ac = None
        bank01_free = [None, None]
        pb2l_free = None
        ltT_dv = None
        xn3 = xn_sb[:].rearrange("p (i t) -> p i t", t=XS)
        for cg in range(4):
            xn_ev = {}
            for tt in range(4 * cg, 4 * cg + 4):
                k = tt % 2
                wait("sync", **merge(xst_free[k]))
                sx = step("sync", "sx", lambda e, tt=tt, k=k: e.dma_start(out=xstage[k][:], in_=x_d[tt * 128:(tt + 1) * 128, :]))
                wait("scalar", sx=sx, gp=gp_const, ac=counts["ac"], dv=counts["dv"])
                ac_s = step("scalar", "ac", lambda e, k=k: e.activation(cen[:], xstage[k][:], AF.Identity, accum_out=st1[:]))
                wait("vector", ac=ac_s)
                step("vector", "dv", lambda e: e.tensor_scalar(st2[:], st1[:], 1.0 / D, None, ALU.mult, op1=ALU.bypass))
                wait("vector", dv=counts["dv"])
                dv_c = step("vector", "dv", lambda e, k=k: e.tensor_scalar(cen[:], xstage[k][:], st2[:, 0:1], None, ALU.subtract, op1=ALU.bypass))
                wait("scalar", dv=dv_c)
                ac_q = step("scalar", "ac", lambda e, k=k: e.activation(xstage[k][:], cen[:], AF.Square, accum_out=st1[:]))
                xst_free[k] = {"ac": ac_q}
                wait("scalar", ac=ac_q)
                ac0 = step("scalar", "ac", lambda e: e.activation(st2[:], st1[:], AF.Ln, bias=eps_c[:, 0:1], scale=1.0 / D))
                wait("scalar", ac=ac0)
                ac = step("scalar", "ac", lambda e: e.activation(st3[:], st2[:], AF.Exp, scale=-0.5))
                p0_ac = ac
                wait("vector", ac=ac, pe=xntok_free)
                dv_xn = step("vector", "dv", lambda e: e.tensor_scalar(xntok[:], cen[:], st3[:, 0:1], None, ALU.mult, op1=ALU.bypass))
                wait("tensor", dv=dv_xn)
                xn_ev[tt] = transp_batch(xntok, None, 16,
                             batch_ap_fn=lambda b0, n8, tt=tt, xn3=xn3: xn3[:, b0:b0 + n8, 1 + tt * 128:1 + (tt + 1) * 128])
                xntok_free = counts["pe"]
            xnready = merge(*[xn_ev[t_] for t_ in xn_ev], *wg1_ready)
            for m in range(4):
                bank = pb[m % 2]
                wait("tensor", **merge(xnready, bank01_free[m % 2]))
                pe = None
                for i in range(TD):
                    pe = step("tensor", "pe", lambda e, bank=bank, i=i, cg=cg, m=m: e.matmul(
                        bank[:], wg1T[m][:, i * 128:(i + 1) * 128],
                        xn_sb[:, i * XS + 1 + cg * 512: i * XS + 1 + (cg + 1) * 512],
                        start=(i == 0), stop=(i == TD - 1)))
                wait("scalar", pe=pe, sx=sx_small)
                ac = step("scalar", "ac", lambda e, bank=bank, m=m, cg=cg: e.activation(
                    h_sb[:, m * 2048 + cg * 512: m * 2048 + (cg + 1) * 512], bank[:],
                    AF.Relu, bias=small[:, BG1 + m:BG1 + m + 1], scale=1.0))
                bank01_free[m % 2] = {"ac": ac}
            wait("tensor", gw=gw_rtr, **merge(pb2l_free))
            pe = None
            for i in range(TD):
                pe = step("tensor", "pe", lambda e, i=i, cg=cg: e.matmul(
                    pb[2][0:16, :], rtr_bf[:, i * 16:(i + 1) * 16],
                    xn_sb[:, i * XS + 1 + cg * 512: i * XS + 1 + (cg + 1) * 512],
                    start=(i == 0), stop=(i == TD - 1)))
            wait("vector", pe=pe)
            ltT_dv = step("vector", "dv", lambda e, cg=cg: e.tensor_copy(
                ltT[:, cg * 512:(cg + 1) * 512], pb[2][0:16, :]))
            pb2l_free = {"dv": ltT_dv}
        p0_done = snap()
        p1_done = p0_done

        if PHASES < 3: raise _Done()
        pb2_free = merge(pb2l_free)
        pT0_cw_free = None
        for tt in range(NTT):
            wait("tensor", **merge({"dv": ltT_dv}, pb2_free))
            pe = step("tensor", "pe", lambda e, tt=tt: e.transpose(
                pb[2][0:128, 0:16], ltT[:, tt * 128:(tt + 1) * 128], ident_f128[0:16, 0:16]))
            wait("vector", pe=pe)
            step("vector", "dv", lambda e: e.tensor_copy(lg[:], pb[2][0:128, 0:16]))
            pb2_free = {"dv": counts["dv"]}
            wait("vector", dv=counts["dv"])
            step("vector", "dv", lambda e: e.tensor_reduce(st1[:], lg[:], mybir.AxisListType.X, ALU.max))
            wait("vector", dv=counts["dv"])
            dv = step("vector", "dv", lambda e: e.tensor_scalar(st2[:], st1[:], -1.0, None, ALU.mult, op1=ALU.bypass))
            wait("scalar", dv=dv)
            ac = step("scalar", "ac", lambda e: e.activation(pp[:], lg[:], AF.Exp, bias=st2[:, 0:1], scale=1.0))
            wait("vector", ac=ac)
            step("vector", "dv", lambda e: e.tensor_copy(rr[:], pp[:]))
            for _ in range(3):
                wait("vector", dv=counts["dv"])
                step("vector", "dv", lambda e: e.tensor_reduce(st1[:], rr[:], mybir.AxisListType.X, ALU.max))
                wait("vector", dv=counts["dv"])
                step("vector", "dv", lambda e: e.scalar_tensor_tensor(rr[:], rr[:], st1[:, 0:1], rr[:], ALU.is_lt, ALU.mult))
            wait("vector", dv=counts["dv"])
            step("vector", "dv", lambda e: e.tensor_reduce(st1[:], rr[:], mybir.AxisListType.X, ALU.max))
            wait("vector", dv=counts["dv"])
            step("vector", "dv", lambda e: e.scalar_tensor_tensor(sel[:], pp[:], st1[:, 0:1], pp[:], ALU.is_ge, ALU.mult))
            wait("vector", dv=counts["dv"])
            step("vector", "dv", lambda e: e.tensor_reduce(st1[:], sel[:], mybir.AxisListType.X, ALU.add))
            wait("vector", dv=counts["dv"])
            step("vector", "dv", lambda e: e.reciprocal(st2[:], st1[:]))
            wait("vector", dv=counts["dv"])
            step("vector", "dv", lambda e: e.tensor_scalar(st3[:], st2[:], 0.8, None, ALU.mult, op1=ALU.bypass))
            wait("vector", dv=counts["dv"])
            dv = step("vector", "dv", lambda e: e.tensor_scalar(cwt_t[:], sel[:], st3[:, 0:1], None, ALU.mult, op1=ALU.bypass))
            wait("tensor", **merge({"dv": dv}, pT0_cw_free))
            pe = step("tensor", "pe", lambda e: e.transpose(pT[0][0:16, 0:128], cwt_t[:], ident_bf[:]))
            wait("vector", pe=pe)
            dv = step("vector", "dv", lambda e, tt=tt: e.tensor_copy(
                cwT_sb[0:16, tt * 128:(tt + 1) * 128], pT[0][0:16, 0:128]))
            pT0_cw_free = {"dv": dv}
        pT_free[0] = merge(pT_free[0], pT0_cw_free)
        p2_done = snap()
        wait("gpsimd", pe=p2_done["pe"], dv=p2_done["dv"], ac=p2_done["ac"])
        gp_cw02 = step("gpsimd", "gp", lambda e: e.memset(cw02[:], 0.2))
        if DEBUG:
            wait("sync", dv=counts["dv"], ac=counts["ac"])
            step("sync", "sx", lambda e: e.dma_start(out=dbg["d_xn"], in_=xn_sb[:]))
            step("sync", "sx", lambda e: e.dma_start(out=dbg["d_h"], in_=h_sb[:]))
            step("sync", "sx", lambda e: e.dma_start(out=dbg["d_cw"], in_=cwT_sb[:]))


        wait("gpsimd", dv=p0_done["dv"])
        cb_gw = step("gpsimd", "gw", lambda e: e.dma_start(
            out=cbrow[0:1, :], in_=convb_d.rearrange("(a b) -> a b", a=1)))
        diag_dv = [None, None]
        diag_pe_free = [None, None]
        pbc_free = [None, None, None]
        cst_free = None
        conv_dma = None
        for i in range(TD):
            ib = i % 2
            wait("vector", **merge(diag_pe_free[ib], {"dv": p0_done["dv"]}))
            for k3 in range(3):
                dv = step("vector", "dv", lambda e, ib=ib, k3=k3, i=i: e.tensor_scalar(
                    diagb[ib * 3 + k3][:], ident_bf[:], convw_sb[:, i * 3 + k3:i * 3 + k3 + 1],
                    None, ALU.mult, op1=ALU.bypass))
            diag_dv[ib] = dv
            wait("tensor", dv=diag_dv[ib], gw=cb_gw)
            for g4 in range(4):
                bk = (i * 4 + g4) % 3
                bank = pb[3 + bk]
                wait("tensor", **merge(pbc_free[bk]))
                pe = None
                for s in range(4):
                    tsub = g4 * 4 + s
                    base = i * XS + 1 + tsub * 128
                    step("tensor", "pe", lambda e, bank=bank, s=s, base=base, ib=ib: e.matmul(
                        bank[:, s * 128:(s + 1) * 128], xn_sb[:, base - 1:base + 127],
                        diagb[ib * 3][:], start=True, stop=False))
                    step("tensor", "pe", lambda e, bank=bank, s=s, base=base, ib=ib: e.matmul(
                        bank[:, s * 128:(s + 1) * 128], xn_sb[:, base:base + 128],
                        diagb[ib * 3 + 1][:], start=False, stop=False))
                    step("tensor", "pe", lambda e, bank=bank, s=s, base=base, ib=ib: e.matmul(
                        bank[:, s * 128:(s + 1) * 128], xn_sb[:, base + 1:base + 129],
                        diagb[ib * 3 + 2][:], start=False, stop=False))
                    pe = step("tensor", "pe", lambda e, bank=bank, s=s, i=i: e.matmul(
                        bank[:, s * 128:(s + 1) * 128], onesrow[:],
                        cbrow[0:1, i * 128:(i + 1) * 128], start=False, stop=True))
                w = {"pe": pe}
                if g4 == 0:
                    w = merge(w, {"sx": cst_free})
                wait("vector", **w)
                dv = step("vector", "dv", lambda e, bank=bank, g4=g4: e.tensor_copy(
                    cstage[:, g4 * 512:(g4 + 1) * 512], bank[:]))
                pbc_free[bk] = {"dv": dv}
            diag_pe_free[ib] = {"pe": counts["pe"]}
            wait("sync", dv=counts["dv"])
            conv_dma = step("sync", "sx", lambda e, i=i: e.dma_start(
                out=convT_scr.rearrange("(q p) (i f) -> p q i f", p=128, f=128)[:, :, i:i + 1, :],
                in_=cstage[:]))
            cst_free = conv_dma
        conv_done_sx = counts["sx"]
        conv_done_pe = counts["pe"]
        if PHASES < 4: raise _Done()
        st = {
            "wuT_ready": [None, None], "wvT_ready": [None, None], "wg2T_ready": [None, None],
            "wuT_free": [p1_done["pe"], p1_done["pe"]],
            "wvT_free": [0, 0], "wg2T_free": [0, 0],
            "scrw": [None, None],
            "rawu_pe": max(raw_pe_free), "rawv_pe": max(raw_pe_free), "wg2raw_pe": [0, 0],
        }
        wuT_scr_w = wuT_scr.rearrange("(b p) (a f) -> a p b f", p=128, f=128)

        def load_and_transpose_j(j):
            k = j % 2
            wait("gpsimd", pe=st["rawu_pe"])
            gw = step("gpsimd", "gw", lambda e, j=j: e.dma_start(out=raw_b[0][:], in_=Wu_d[j * 128:(j + 1) * 128, :]))
            wait("tensor", gw=gw)
            pre = merge({"pe": st["wuT_free"][k]}, st["scrw"][k])
            ev = transp_batch(raw_b[0], None, 16, pre_evac=pre,
                              batch_ap_fn=lambda b0, n8, k=k: wuT_b[k][:, b0 * 128:(b0 + n8) * 128])
            st["rawu_pe"] = counts["pe"]
            st["wuT_ready"][k] = ev
            wait("sync", **ev)
            st["scrw"][k] = {"sx": step("sync", "sx", lambda e, k=k, j=j: e.dma_start(
                out=wuT_scr_w[j:j + 1], in_=wuT_b[k][:]))}
            wait("gpsimd", pe=st["rawv_pe"])
            gw = step("gpsimd", "gw", lambda e, j=j: e.dma_start(out=raw_b[1][:], in_=Wv_d[j * 128:(j + 1) * 128, :]))
            wait("tensor", gw=gw)
            ev = transp_batch(raw_b[1], None, 16, pre_evac={"pe": st["wvT_free"][k]},
                              batch_ap_fn=lambda b0, n8, k=k: wvT_b[k][:, b0 * 128:(b0 + n8) * 128])
            st["rawv_pe"] = counts["pe"]
            st["wvT_ready"][k] = ev
            wait("gpsimd", pe=st["wg2raw_pe"][k])
            gw = step("gpsimd", "gw", lambda e, j=j, k=k: e.dma_start(out=wg2raw_b[k][:], in_=Wg2_d[j * 128:(j + 1) * 128, :]))
            wait("tensor", gw=gw)
            ev = transp_batch(wg2raw_b[k], None, 4, pre_evac={"pe": st["wg2T_free"][k]},
                              batch_ap_fn=lambda b0, n8, k=k: wg2T_b[k][:, b0 * 128:(b0 + n8) * 128])
            st["wg2raw_pe"][k] = counts["pe"]
            st["wg2T_ready"][k] = ev

        ubank_free = [merge(bank01_free[0]), merge(bank01_free[1])]
        vbank_free = [merge(pb2_free), merge(pbc_free[0])]
        gbank_free = [merge(pbc_free[1]), merge(pbc_free[2])]
        gates_rd = None
        wait("vector", pe=conv_done_pe)
        wait("scalar", sx=conv_done_sx)
        load_and_transpose_j(0)
        for j in range(NJ):
            k = j % 2
            wait("tensor", **merge(st["wuT_ready"][k], st["wvT_ready"][k], st["wg2T_ready"][k]))
            for c in range(4):
                gb, ub, vb = pb[4 + (c % 2)], pb[c % 2], pb[2 + (c % 2)]
                wait("tensor", **merge(gbank_free[c % 2]))
                pe = None
                for m in range(4):
                    pe = step("tensor", "pe", lambda e, gb=gb, m=m, c=c, k=k: e.matmul(
                        gb[:], wg2T_b[k][:, m * 128:(m + 1) * 128],
                        h_sb[:, m * 2048 + c * 512: m * 2048 + (c + 1) * 512],
                        start=(m == 0), stop=(m == 3)))
                wait("scalar", pe=pe, dv=gates_rd)
                ac_g = step("scalar", "ac", lambda e, gb=gb, j=j, c=c: e.activation(
                    gates[:, c * 512:(c + 1) * 512], gb[:], AF.Sigmoid,
                    bias=small[:, BG2 + j:BG2 + j + 1], scale=1.0))
                gbank_free[c % 2] = {"ac": ac_g}
                wait("tensor", **merge(ubank_free[c % 2]))
                pe = None
                for i in range(TD):
                    pe = step("tensor", "pe", lambda e, ub=ub, i=i, c=c, k=k: e.matmul(
                        ub[:], wuT_b[k][:, i * 128:(i + 1) * 128],
                        xn_sb[:, i * XS + 1 + c * 512: i * XS + 1 + (c + 1) * 512],
                        start=(i == 0), stop=(i == TD - 1)))
                wait("vector", pe=pe, ac=ac_g)
                dv_u = step("vector", "dv", lambda e, ub=ub, j=j, c=c: e.scalar_tensor_tensor(
                    u_g[:, c * 512:(c + 1) * 512], ub[:], small[:, BU + j:BU + j + 1],
                    gates[:, c * 512:(c + 1) * 512], ALU.add, ALU.mult))
                ubank_free[c % 2] = {"dv": dv_u}
                wait("tensor", **merge(vbank_free[c % 2]))
                pe = None
                for i in range(TD):
                    pe = step("tensor", "pe", lambda e, vb=vb, i=i, c=c, k=k: e.matmul(
                        vb[:], wvT_b[k][:, i * 128:(i + 1) * 128],
                        xn_sb[:, i * XS + 1 + c * 512: i * XS + 1 + (c + 1) * 512],
                        start=(i == 0), stop=(i == TD - 1)))
                wait("vector", pe=pe)
                dv_v = step("vector", "dv", lambda e, vb=vb, j=j, c=c: e.scalar_tensor_tensor(
                    v_g[:, c * 512:(c + 1) * 512], vb[:], small[:, BV + j:BV + j + 1],
                    gates[:, c * 512:(c + 1) * 512], ALU.add, ALU.mult))
                vbank_free[c % 2] = {"dv": dv_v}
            st["wuT_free"][k] = counts["pe"]
            st["wvT_free"][k] = counts["pe"]
            st["wg2T_free"][k] = counts["pe"]
            if j + 1 < NJ:
                load_and_transpose_j(j + 1)
            wait("vector", dv=counts["dv"])
            step("vector", "dv", lambda e: e.tensor_tensor_scan(
                cum[:], u_g[:], u_g[:], 0.0, ALU.add, ALU.bypass))
            wait("vector", dv=counts["dv"])
            dv_go = step("vector", "dv", lambda e, j=j: e.tensor_tensor(
                go_sb[:, j * 2048:(j + 1) * 2048], cum[:], v_g[:], ALU.mult))
            gates_rd = dv_go
            if DEBUG and j == 0:
                wait("sync", dv=dv_go)
                step("sync", "sx", lambda e: e.dma_start(out=dbg["d_ug"], in_=u_g[:]))
                step("sync", "sx", lambda e: e.dma_start(out=dbg["d_cum"], in_=cum[:]))
        p3_done = snap()
        if DEBUG:
            wait("sync", dv=counts["dv"])
            step("sync", "sx", lambda e: e.dma_start(out=dbg["d_go"], in_=go_sb[:]))

        if PHASES < 5: raise _Done()
        for engn in ("sync", "gpsimd", "scalar"):
            wait(engn, pe=p3_done["pe"], dv=p3_done["dv"], ac=p3_done["ac"], sx=p3_done["sx"])
        wuT_scr_rc = wuT_scr.rearrange("(b p) d -> p b d", p=128)
        p4_ready = [None, None]
        p4_free = [None, None]
        p4x_ready = [None] * 4
        p4x_free = [None] * 4
        pb_free4 = [merge(ubank_free[0]), merge(ubank_free[1]), merge(vbank_free[0]),
                    merge(vbank_free[1]), merge(gbank_free[0]), merge(gbank_free[1])]
        BMAP = [[0, 1, 2, 3], [4, 5, 0, 1]]

        def p4_load(idx):
            kk = idx % 2
            pr, rem = divmod(idx, 4)
            dch, piece = divmod(rem, 2)
            engn = "sync" if kk == 0 else "gpsimd"
            semk = "sx" if engn == "sync" else "gw"
            wait(engn, **merge(p4_free[kk]))
            p4_ready[kk] = {semk: step(engn, semk, lambda e, piece=piece, dch=dch, kk=kk: e.dma_start(
                out=p4wuT[kk][:],
                in_=wuT_scr_rc[:, piece * 8:(piece + 1) * 8, dch * 1024:(dch + 1) * 1024]))}

        def p4_loadx(tt):
            sl = tt % 4
            wait("scalar", **merge(p4x_free[sl]))
            p4x_ready[sl] = step("scalar", "ax", lambda e, tt=tt, sl=sl: e.dma_start(
                out=p4ax[sl][:], in_=x_d[tt * 128:(tt + 1) * 128, :]))

        p4_load(0); p4_load(1)
        p4_loadx(0); p4_loadx(1); p4_loadx(2); p4_loadx(3)
        for pr in range(8):
            ax_dv = [None, None]
            for dch in range(2):
                for piece in range(2):
                    idx = pr * 4 + dch * 2 + piece
                    kk = idx % 2
                    wait("tensor", **merge(p4_ready[kk]))
                    for jo in range(8):
                        j = piece * 8 + jo
                        for tq in range(2):
                            tt = 2 * pr + tq
                            for dq in range(2):
                                bank = pb[BMAP[dch][tq * 2 + dq]]
                                if j == 0:
                                    wait("tensor", **merge(pb_free4[BMAP[dch][tq * 2 + dq]]))
                                step("tensor", "pe", lambda e, j=j, jo=jo, dq=dq, tt=tt, kk=kk, bank=bank: e.matmul(
                                    bank[:], go_sb[:, j * 2048 + tt * 128: j * 2048 + (tt + 1) * 128],
                                    p4wuT[kk][:, jo * 1024 + dq * 512: jo * 1024 + (dq + 1) * 512],
                                    start=(j == 0), stop=(j == NJ - 1)))
                    p4_free[kk] = {"pe": counts["pe"]}
                    if idx + 2 < 32:
                        p4_load(idx + 2)
                pe_done = counts["pe"]
                for tq in range(2):
                    tt = 2 * pr + tq
                    sl = tt % 4
                    wait("vector", pe=pe_done, ax=p4x_ready[sl])
                    for dq in range(2):
                        bk = BMAP[dch][tq * 2 + dq]
                        dv = step("vector", "dv", lambda e, sl=sl, dch=dch, dq=dq, bk=bk: e.tensor_tensor(
                            p4ax[sl][:, dch * 1024 + dq * 512: dch * 1024 + (dq + 1) * 512],
                            p4ax[sl][:, dch * 1024 + dq * 512: dch * 1024 + (dq + 1) * 512],
                            pb[bk][:], ALU.add))
                        pb_free4[bk] = {"dv": dv}
                    ax_dv[tq] = dv if dch == 1 else None
            for tq in range(2):
                tt = 2 * pr + tq
                sl = tt % 4
                wait("sync", dv=counts["dv"])
                sxs = step("sync", "sx", lambda e, tt=tt, sl=sl: e.dma_start(
                    out=attnx_scr[tt * 128:(tt + 1) * 128, :], in_=p4ax[sl][:]))
                p4x_free[sl] = {"sx": sxs}
            if 2 * pr + 4 < NTT:
                p4_loadx(2 * pr + 4)
            if 2 * pr + 5 < NTT:
                p4_loadx(2 * pr + 5)
        p4_done = snap()
        if DEBUG:
            wait("sync", sx=counts["sx"])
            step("sync", "sx", lambda e: e.dma_start(out=dbg["d_ax"], in_=attnx_scr[:, :]))

        if PHASES < 6: raise _Done()
        for engn in ENGS:
            wait(engn, **{k: v for k, v in p4_done.items() if v})

        NU = E + 4
        out_store = [[None] * 8, [None] * 8]
        we1_ready = [None, None]; we1_free = [None, None]
        we2_ready = [None, None]; we2_free = [None, None]
        cwb_ready = [None, None]
        cwrow_pe_free = None
        gB = [None, None, None, None]
        wB = [None, None]
        acc_wr = None

        we1_r = we1_d.rearrange("(e n p) h -> e p n h", n=TD, p=128)
        we3_r = we3_d.rearrange("(e n p) h -> e p n h", n=TD, p=128)
        we2_r = we2_d.rearrange("(e n p) d -> e p n d", n=8, p=128)
        ws1_r = ws1_d.rearrange("(n p) h -> p n h", p=128)
        ws3_r = ws3_d.rearrange("(n p) h -> p n h", p=128)
        ws2_r = ws2_d.rearrange("(m n p) d -> m p n d", m=4, p=128)

        def w1_ap(u, q, which):
            if u < E:
                rr_ = we1_r if which == "w1" else we3_r
                return rr_[u:u + 1, :, :, q * 256:(q + 1) * 256]
            qq = u - E
            rr_ = ws1_r if which == "w1" else ws3_r
            return rr_[:, :, qq * 1024 + q * 256: qq * 1024 + (q + 1) * 256]

        def w2_ap(u, dc):
            if u < E:
                return we2_r[u:u + 1, :, :, dc * 512:(dc + 1) * 512]
            return ws2_r[u - E:u - E + 1, :, :, dc * 512:(dc + 1) * 512]

        acc_tb_done = [[None] * 8, [None] * 8]

        def emit_C(half):
            sx0 = counts["sx"]
            for tb in range(8):
                tt = half * 8 + tb
                wait("gpsimd", **merge(acc_tb_done[half][tb], {"sx": sx0}))
                step("gpsimd", "gw", lambda e, tb=tb, tt=tt: e.dma_start(
                    out=acc[:, tb * 2048:(tb + 1) * 2048],
                    in_=attnx_scr[tt * 128:(tt + 1) * 128, :], accum_op=ALU.add))
                g2 = step("gpsimd", "gw", lambda e, tb=tb, tt=tt: e.dma_start(
                    out=acc[:, tb * 2048:(tb + 1) * 2048],
                    in_=convT_scr[tt * 128:(tt + 1) * 128, :], accum_op=ALU.add))
                wait("sync", gw=g2)
                out_store[half][tb] = step("sync", "sx", lambda e, tb=tb, tt=tt: e.dma_start(
                    out=out_d[tt * 128:(tt + 1) * 128, :], in_=acc[:, tb * 2048:(tb + 1) * 2048]))
                yield

        def emit_half(half, pump=None):
            nonlocal cwrow_pe_free, acc_wr

            late_gen = [None]

            def do_pump():
                if pump is not None:
                    next(pump, None)
            w1seq = [(u, q, w) for u in range(NU) for q in range(4) for w in ("w1", "w3")]
            w2seq = [(u, dc) for u in range(NU) for dc in range(4)]

            def load_w1(idx):
                kk = idx % 2
                u, q, which = w1seq[idx]
                wait("gpsimd", **merge(we1_free[kk]))
                we1_ready[kk] = {"gw": step("gpsimd", "gw", lambda e, u=u, q=q, which=which, kk=kk: e.dma_start(
                    out=we1buf[kk][:], in_=w1_ap(u, q, which)))}

            def load_w2(idx):
                kk = idx % 2
                u, dc = w2seq[idx]
                wait("gpsimd", **merge(we2_free[kk]))
                we2_ready[kk] = {"gw": step("gpsimd", "gw", lambda e, u=u, dc=dc, kk=kk: e.dma_start(
                    out=we2buf[kk][:], in_=w2_ap(u, dc)))}

            load_w1(0); load_w1(1); load_w2(0); load_w2(1)
            cw_sx = [None, None]

            def load_cwrow(u):
                sl = u % 2
                wait("sync", pe=cwrow_pe_free)
                cw_sx[sl] = step("sync", "sx", lambda e, u=u, sl=sl, half=half: e.dma_start(
                    out=cwrow[0:1, sl * 1024:(sl + 1) * 1024],
                    in_=cwT_sb[u:u + 1, half * 1024:(half + 1) * 1024]))

            load_cwrow(0)
            for u in range(NU):
                kk_cw = u % 2
                if u + 1 < E:
                    load_cwrow(u + 1)
                if u < E:
                    sl = u % 2
                    wait("tensor", sx=cw_sx[sl])
                    for cc in range(2):
                        wait("tensor", **merge(gB[cc]))
                    pe = None
                    for cc in range(2):
                        pe = step("tensor", "pe", lambda e, cc=cc, sl=sl: e.matmul(
                            pb[cc][:], onesrow[:], cwrow[0:1, sl * 1024 + cc * 512: sl * 1024 + (cc + 1) * 512],
                            start=True, stop=True))
                    cwrow_pe_free = pe
                    dv = None
                    wait("vector", pe=pe)
                    for cc in range(2):
                        dv = step("vector", "dv", lambda e, cc=cc, kk_cw=kk_cw: e.tensor_copy(
                            cwbb[kk_cw][:, cc * 512:(cc + 1) * 512], pb[cc][:]))
                        gB[cc] = {"dv": dv}
                    cwb_ready[kk_cw] = {"dv": dv}
                for q in range(4):
                    idx = (u * 4 + q) * 2
                    kk1, kk3 = idx % 2, (idx + 1) % 2
                    wait("tensor", **merge(we1_ready[kk1]))
                    for c in range(2):
                        for mp in range(2):
                            bank = pb[mp * 2 + c]
                            wait("tensor", **merge(gB[mp * 2 + c]))
                            pe = None
                            for i in range(TD):
                                pe = step("tensor", "pe", lambda e, bank=bank, i=i, mp=mp, c=c, kk1=kk1, half=half: e.matmul(
                                    bank[:], we1buf[kk1][:, i * 256 + mp * 128: i * 256 + (mp + 1) * 128],
                                    xn_sb[:, i * XS + 1 + half * 1024 + c * 512: i * XS + 1 + half * 1024 + (c + 1) * 512],
                                    start=(i == 0), stop=(i == TD - 1)))
                            wait("scalar", pe=pe)
                            ac = step("scalar", "ac", lambda e, bank=bank, mp=mp, c=c: e.activation(
                                g1pair[:, mp * 1024 + c * 512: mp * 1024 + (c + 1) * 512], bank[:], AF.Silu))
                            gB[mp * 2 + c] = {"ac": ac}
                    we1_free[kk1] = {"pe": counts["pe"]}
                    if idx + 2 < len(w1seq):
                        load_w1(idx + 2)
                    do_pump()
                    wait("tensor", **merge(we1_ready[kk3]))
                    for c in range(2):
                        for mp in range(2):
                            bank = pb[mp * 2 + c]
                            wait("tensor", **merge(gB[mp * 2 + c]))
                            pe = None
                            for i in range(TD):
                                pe = step("tensor", "pe", lambda e, bank=bank, i=i, mp=mp, c=c, kk3=kk3, half=half: e.matmul(
                                    bank[:], we1buf[kk3][:, i * 256 + mp * 128: i * 256 + (mp + 1) * 128],
                                    xn_sb[:, i * XS + 1 + half * 1024 + c * 512: i * XS + 1 + half * 1024 + (c + 1) * 512],
                                    start=(i == 0), stop=(i == TD - 1)))
                            cwbuf = cw02 if u >= E else cwbb[kk_cw]
                            w = {"pe": pe}
                            if u < E:
                                w = merge(w, cwb_ready[kk_cw])
                            else:
                                w = merge(w, {"gp": gp_cw02})
                            wait("vector", **w)
                            step("vector", "dv", lambda e, bank=bank, c=c, cwbuf=cwbuf, half=half: e.tensor_tensor(
                                tmp512[:], bank[:], cwbuf[:, c * 512:(c + 1) * 512], ALU.mult))
                            dv = step("vector", "dv", lambda e, q=q, mp=mp, c=c: e.tensor_tensor(
                                g_e[:, (q * 2 + mp) * 1024 + c * 512: (q * 2 + mp) * 1024 + (c + 1) * 512],
                                tmp512[:], g1pair[:, mp * 1024 + c * 512: mp * 1024 + (c + 1) * 512], ALU.mult))
                            gB[mp * 2 + c] = {"dv": dv}
                    we1_free[kk3] = {"pe": counts["pe"]}
                    if idx + 3 < len(w1seq):
                        load_w1(idx + 3)
                    do_pump()
                ge_dv = counts["dv"]
                for dc in range(4):
                    idx2 = u * 4 + dc
                    kk2 = idx2 % 2
                    if u == NU - 1 and dc == 3 and late_gen[0] is None:
                        late_gen[0] = emit_C(half)
                    wait("tensor", **merge({"dv": ge_dv}, we2_ready[kk2]))
                    for tb in range(8):
                        bank = pb[4 + (tb % 2)]
                        wait("tensor", **merge(wB[tb % 2]))
                        pe = None
                        for hb in range(8):
                            pe = step("tensor", "pe", lambda e, bank=bank, hb=hb, tb=tb, kk2=kk2: e.matmul(
                                bank[:], g_e[:, hb * 1024 + tb * 128: hb * 1024 + (tb + 1) * 128],
                                we2buf[kk2][:, hb * 512:(hb + 1) * 512],
                                start=(hb == 0), stop=(hb == 7)))
                        w2w = {"pe": pe}
                        if u == 0 and half == 1:
                            w2w = merge(w2w, {"sx": out_store[0][tb]})
                        wait("vector", **w2w)
                        if u == 0:
                            dv = step("vector", "dv", lambda e, bank=bank, tb=tb, dc=dc: e.tensor_copy(
                                acc[:, tb * 2048 + dc * 512: tb * 2048 + (dc + 1) * 512], bank[:]))
                        else:
                            dv = step("vector", "dv", lambda e, bank=bank, tb=tb, dc=dc: e.tensor_tensor(
                                acc[:, tb * 2048 + dc * 512: tb * 2048 + (dc + 1) * 512],
                                acc[:, tb * 2048 + dc * 512: tb * 2048 + (dc + 1) * 512], bank[:], ALU.add))
                        wB[tb % 2] = {"dv": dv}
                        if u == NU - 1 and dc == 3:
                            acc_tb_done[half][tb] = {"dv": dv}
                            if late_gen[0] is not None and tb < 2:
                                next(late_gen[0], None)
                    we2_free[kk2] = {"pe": counts["pe"]}
                    if idx2 + 2 < len(w2seq):
                        load_w2(idx2 + 2)
            acc_wr = counts["dv"]
            if DEBUG and half == 0:
                wait("sync", dv=acc_wr)
                step("sync", "sx", lambda e: e.dma_start(out=dbg["d_acc"], in_=acc[:]))

            return late_gen[0]

        genC0 = emit_half(0)
        if PHASES < 7:
            for _ in genC0:
                pass
            raise _Done()
        h0 = snap()
        for engn in ("tensor", "vector", "scalar"):
            wait(engn, pe=h0["pe"], dv=h0["dv"], ac=h0["ac"])
        genC1 = emit_half(1, pump=genC0)
        for _ in genC0:
            pass
        for _ in genC1:
            pass

    except _Done:
        pass
    fin = dict(counts)

    def _finwait(eng):
        if fin["sx"]:
            eng.wait_ge(semSX, 16 * fin["sx"])
        if fin["gw"]:
            eng.wait_ge(semGW, 16 * fin["gw"])
        if fin.get("ax"):
            eng.wait_ge(semAX, 16 * fin["ax"])
    streams["sync"].append(_finwait)

    _replay(block, streams)
    stk.close()
    return nc


_NC_CACHE = None


def kernel(L0_attn_x, L0_attn_Wu, L0_attn_bu, L0_attn_Wv, L0_attn_bv,
           L0_attn_Wg1, L0_attn_bg1, L0_attn_Wg2, L0_attn_bg2,
           L0_attn_conv_w, L0_attn_conv_b,
           L0_ffn_w_router, L0_ffn_we1, L0_ffn_we3, L0_ffn_we2,
           L0_ffn_ws1, L0_ffn_ws3, L0_ffn_ws2):
    global _NC_CACHE
    if _NC_CACHE is None:
        _NC_CACHE = build_nc()
    nc = _NC_CACHE
    f32 = np.float32
    base = {
        "Wu": np.ascontiguousarray(L0_attn_Wu, f32), "bu": np.ascontiguousarray(L0_attn_bu, f32),
        "Wv": np.ascontiguousarray(L0_attn_Wv, f32), "bv": np.ascontiguousarray(L0_attn_bv, f32),
        "Wg1": np.ascontiguousarray(L0_attn_Wg1, f32), "bg1": np.ascontiguousarray(L0_attn_bg1, f32),
        "Wg2": np.ascontiguousarray(L0_attn_Wg2, f32), "bg2": np.ascontiguousarray(L0_attn_bg2, f32),
        "conv_w": np.ascontiguousarray(np.asarray(L0_attn_conv_w, f32).reshape(D, 3)),
        "conv_b": np.ascontiguousarray(L0_attn_conv_b, f32),
        "w_router": np.ascontiguousarray(L0_ffn_w_router, f32),
        "we1": np.ascontiguousarray(np.asarray(L0_ffn_we1, f32).reshape(E * D, H)),
        "we3": np.ascontiguousarray(np.asarray(L0_ffn_we3, f32).reshape(E * D, H)),
        "we2": np.ascontiguousarray(np.asarray(L0_ffn_we2, f32).reshape(E * H, D)),
        "ws1": np.ascontiguousarray(L0_ffn_ws1, f32), "ws3": np.ascontiguousarray(L0_ffn_ws3, f32),
        "ws2": np.ascontiguousarray(L0_ffn_ws2, f32),
    }
    in_maps = []
    for b in range(8):
        m = dict(base)
        m["x"] = np.ascontiguousarray(L0_attn_x[b], f32)
        in_maps.append(m)
    res = run_bass_kernel_spmd(nc, in_maps, core_ids=list(range(8)))
    return np.stack([np.asarray(r["out"], dtype=np.float32) for r in res.results], axis=0)


# version 11
# speedup vs baseline: 1.1321x; 1.1321x over previous
import numpy as np
import concourse.bass as bass
import concourse.mybir as mybir
from concourse.bass_utils import run_bass_kernel_spmd
from concourse.alu_op_type import AluOpType

F32 = mybir.dt.float32
BF16 = mybir.dt.bfloat16
AF = mybir.ActivationFunctionType
ALU = AluOpType

D = 2048; L = 2048; R = 2048; DH = 512; E = 16; H = 1024; HS = 4096
XS = 2050
TD = 16; NJ = 16; NTT = 16
DEBUG = False


SIM_NO_RACE = False
PHASES = 99


class _Done(Exception):
    pass


def _replay(block, streams):
    @block.sync
    def _(eng):
        for f in streams["sync"]:
            f(eng)

    @block.gpsimd
    def _(eng):
        for f in streams["gpsimd"]:
            f(eng)

    @block.tensor
    def _(eng):
        for f in streams["tensor"]:
            f(eng)

    @block.vector
    def _(eng):
        for f in streams["vector"]:
            f(eng)

    @block.scalar
    def _(eng):
        for f in streams["scalar"]:
            f(eng)


def build_nc():
    nc = bass.Bass("TRN2", target_bir_lowering=False,
                   detect_race_conditions=not SIM_NO_RACE)

    def din(name, shape):
        return nc.dram_tensor(name, shape, F32, kind="ExternalInput").ap()
    x_d = din("x", [L, D])
    Wu_d = din("Wu", [R, D]); bu_d = din("bu", [R])
    Wv_d = din("Wv", [R, D]); bv_d = din("bv", [R])
    Wg1_d = din("Wg1", [DH, D]); bg1_d = din("bg1", [DH])
    Wg2_d = din("Wg2", [R, DH]); bg2_d = din("bg2", [R])
    convw_d = din("conv_w", [D, 3]); convb_d = din("conv_b", [D])
    wr_d = din("w_router", [D, E])
    we1_d = din("we1", [E * D, H]); we3_d = din("we3", [E * D, H]); we2_d = din("we2", [E * H, D])
    ws1_d = din("ws1", [D, HS]); ws3_d = din("ws3", [D, HS]); ws2_d = din("ws2", [HS, D])
    out_d = nc.dram_tensor("out", [L, D], F32, kind="ExternalOutput").ap()
    wuT_scr = nc.dram_tensor("wuT_scr", [R, D], BF16, kind="Internal").ap()
    attnx_scr = nc.dram_tensor("attnx_scr", [L, D], F32, kind="Internal").ap()
    convT_scr = nc.dram_tensor("convT_scr", [L, D], F32, kind="Internal").ap()
    dbg = {}
    if DEBUG:
        for nm, shape, dt in [("d_xn", [128, TD * XS], BF16), ("d_h", [128, 4 * 2048], BF16),
                              ("d_go", [128, NJ * 2048], BF16), ("d_cw", [16, 2048], BF16),
                              ("d_acc", [128, 8 * 2048], F32), ("d_ug", [128, 2048], F32),
                              ("d_cum", [128, 2048], F32), ("d_ax", [L, D], F32)]:
            dbg[nm] = nc.dram_tensor(nm, shape, dt, kind="ExternalOutput").ap()

    BASE = 16512
    def at(name, shape, dtype, off):
        return nc.alloc_sbuf_tensor_at(name, shape, dtype, offset=BASE + off)

    xn_sb = at("xn_sb", [128, TD * XS], BF16, 0)
    ident_bf = at("ident_bf", [128, 128], BF16, 65600)
    ident_f128 = at("ident_f128", [128, 128], F32, 65856)
    onesrow = at("onesrow", [1, 128], BF16, 66368)
    small = at("small", [128, 112], F32, 66624)
    convw_sb = at("convw_sb", [128, 48], F32, 67072)
    cwT_sb = at("cwT_sb", [16, 2048], BF16, 67264)
    cwrow = at("cwrow", [1, 2048], BF16, 71360)
    RA = 75456
    TOP = RA + 135168
    rtr_bf = at("rtr_bf", [128, 256], BF16, TOP)
    lg = at("lg", [128, 16], F32, TOP + 512)
    pp = at("pp", [128, 16], F32, TOP + 576)
    rr = at("rr", [128, 16], F32, TOP + 640)
    sel = at("sel", [128, 16], F32, TOP + 704)
    cwt_t = at("cwt_t", [128, 16], BF16, TOP + 768)
    cw02 = at("cw02", [128, 1024], BF16, TOP)
    go_sb = at("go_sb", [128, NJ * 2048], BF16, RA)
    h_sb = at("h_sb", [128, 4 * 2048], BF16, RA + 65536)
    wuT_b = [at(f"wuT{k}", [128, 2048], BF16, RA + 81920 + 4096 * k) for k in range(2)]
    wvT_b = [at(f"wvT{k}", [128, 2048], BF16, RA + 90112 + 4096 * k) for k in range(2)]
    wg2T_b = [at(f"wg2T{k}", [128, 512], BF16, RA + 98304 + 1024 * k) for k in range(2)]
    raw_b = [at(f"raw{k}", [128, 2048], BF16, RA + 100352 + 4096 * k) for k in range(2)]
    wg2raw_b = [at(f"wg2raw{k}", [128, 512], BF16, RA + 108544 + 1024 * k) for k in range(2)]
    u_g = at("u_g", [128, 2048], F32, RA + 110592)
    cum = at("cum", [128, 2048], F32, RA + 118784)
    gates = at("gates", [128, 2048], BF16, RA + 126976)
    v_g = at("v_g", [128, 2048], BF16, RA + 131072)
    xstage = [at("xst0", [128, 2048], F32, RA + 110592),
              at("xst1", [128, 2048], F32, RA + 118784)]
    cen = at("cen", [128, 2048], F32, RA + 126976)
    xntok = at("xntok", [128, 2048], BF16, RA + 81920)
    ltT = at("ltT", [16, 2048], F32, RA + 90112)
    st1 = at("st1", [128, 1], F32, 66624 + 320)
    st2 = at("st2", [128, 1], F32, 66624 + 352)
    st3 = at("st3", [128, 1], F32, 66624 + 384)
    eps_c = at("eps_c", [128, 1], F32, 66624 + 416)
    diagb = [at(f"diag{k}", [128, 128], BF16, RA + 118784 + 256 * k) for k in range(6)]
    cbrow = at("cbrow", [1, 2048], BF16, RA + 110592)
    cstage = at("cstage", [128, 2048], F32, RA + 126976)
    p4wuT = [at("p4wuT0", [128, 8192], BF16, RA + 65536),
             at("p4wuT1", [128, 8192], BF16, RA + 81920)]
    p4ax = [at("p4ax0", [128, 2048], F32, RA + 126976),
            at("p4ax1", [128, 2048], F32, RA + 100352),
            at("p4ax2", [128, 2048], F32, RA + 110592),
            at("p4ax3", [128, 2048], F32, RA + 118784)]
    acc = at("acc", [128, 8 * 2048], F32, RA)
    g_e = at("g_e", [128, 8 * 1024], BF16, RA + 65536)
    g1pair = at("g1pair", [128, 2048], F32, RA + 81920)
    tmp512 = at("tmp512", [128, 512], F32, RA + 90112)
    we1buf = [at(f"we1b{k}", [128, 4096], BF16, RA + 92160 + 8192 * k) for k in range(2)]
    we2buf = [at(f"we2b{k}", [128, 4096], BF16, RA + 108544 + 8192 * k) for k in range(2)]
    cwbb = [at(f"cwb{k}", [128, 1024], BF16, RA + 124928 + 2048 * k) for k in range(2)]
    BU, BV, BG2, CB, BG1 = 0, 16, 32, 48, 64

    import contextlib
    stk = contextlib.ExitStack()
    pb = [stk.enter_context(nc.psum_tensor(f"pb{k}", [128, 512], F32)) for k in range(6)]
    pT = [stk.enter_context(nc.psum_tensor(f"pT{k}", [128, 1024], BF16)) for k in range(2)]
    semGW = stk.enter_context(nc.semaphore("semGW"))
    semSX = stk.enter_context(nc.semaphore("semSX"))
    semPE = stk.enter_context(nc.semaphore("semPE"))
    semDV = stk.enter_context(nc.semaphore("semDV"))
    semAC = stk.enter_context(nc.semaphore("semAC"))
    semGP = stk.enter_context(nc.semaphore("semGP"))
    semAX = stk.enter_context(nc.semaphore("semAX"))
    block = stk.enter_context(nc.Block())

    SEMS = {"gw": semGW, "sx": semSX, "pe": semPE, "dv": semDV, "ac": semAC, "gp": semGP, "ax": semAX}
    DMA_SEMS = {"gw", "sx", "ax"}
    counts = {k: 0 for k in SEMS}
    ENGS = ["sync", "gpsimd", "tensor", "vector", "scalar"]
    waited = {e: {} for e in ENGS}
    streams = {e: [] for e in ENGS}

    def emit(engn, fn):
        streams[engn].append(fn)

    def wait(engn, **need):
        todo = [(k, v) for k, v in need.items() if v and v > waited[engn].get(k, 0)]
        if not todo:
            return
        for k, v in todo:
            waited[engn][k] = v
        def f(eng, todo=tuple(todo)):
            for k, v in todo:
                eng.wait_ge(SEMS[k], 16 * v if k in DMA_SEMS else v)
        emit(engn, f)

    def step(engn, semkey, buildfn):
        counts[semkey] += 1
        amt = 16 if semkey in DMA_SEMS else 1
        def f(eng, buildfn=buildfn, amt=amt, semkey=semkey):
            buildfn(eng).then_inc(SEMS[semkey], amt)
        emit(engn, f)
        return counts[semkey]

    def raw(engn, buildfn):
        emit(engn, lambda eng, b=buildfn: b(eng))

    def snap():
        return dict(counts)

    def merge(*deps):
        out = {}
        for d in deps:
            if not d:
                continue
            for k, v in d.items():
                if v:
                    out[k] = max(out.get(k, 0), v)
        return out

    pT_free = [None, None]
    evac_eng = ["vector", "scalar"]

    def transp_batch(src_buf, dst_off_fn, nt, pre_evac=None, batch_ap_fn=None):
            ev_all = {}
            first = [True, True]
            for b0 in range(0, nt, 8):
                kb = (b0 // 8) % 2
                bank = pT[kb]
                wait("tensor", **merge(pT_free[kb]))
                n8 = min(8, nt - b0)
                pe = None
                for s in range(n8):
                    i = b0 + s
                    pe = step("tensor", "pe", lambda e, bank=bank, s=s, i=i, src=src_buf: e.transpose(
                        bank[:, s * 128:(s + 1) * 128], src[:, i * 128:(i + 1) * 128], ident_bf[:]))
                en = evac_eng[kb]
                key = "dv" if en == "vector" else "ac"
                w = {"pe": pe}
                if pre_evac and first[kb]:
                    w = merge(w, pre_evac)
                first[kb] = False
                wait(en, **w)
                if batch_ap_fn is not None:
                    out_ap = batch_ap_fn(b0, n8)
                    if en == "vector":
                        c = step(en, key, lambda e, bank=bank, n8=n8, out_ap=out_ap: e.tensor_copy(
                            out_ap, bank[:, 0:n8 * 128]))
                    else:
                        c = step(en, key, lambda e, bank=bank, n8=n8, out_ap=out_ap: e.copy(
                            out_ap, bank[:, 0:n8 * 128]))
                else:
                    c = None
                    for s in range(n8):
                        i = b0 + s
                        dst, off = dst_off_fn(i)
                        if en == "vector":
                            c = step(en, key, lambda e, bank=bank, s=s, dst=dst, off=off: e.tensor_copy(
                                dst[:, off:off + 128], bank[:, s * 128:(s + 1) * 128]))
                        else:
                            c = step(en, key, lambda e, bank=bank, s=s, dst=dst, off=off: e.copy(
                                dst[:, off:off + 128], bank[:, s * 128:(s + 1) * 128]))
                evh = {key: c}
                pT_free[kb] = evh
                ev_all = merge(ev_all, evh)
            return ev_all

    raw("gpsimd", lambda e: e.memset(xn_sb[:], 0.0))
    g0 = step("gpsimd", "gp", lambda e: e.memset(ident_bf[:], 0.0))
    wait("gpsimd", gp=g0)
    raw("gpsimd", lambda e: e.affine_select(out=ident_bf[:], in_=ident_bf[:],
        compare_op=ALU.not_equal, fill=1.0, base=0, pattern=[[-1, 128]], channel_multiplier=1))
    g0 = step("gpsimd", "gp", lambda e: e.memset(ident_f128[:], 0.0))
    wait("gpsimd", gp=g0)
    raw("gpsimd", lambda e: e.affine_select(out=ident_f128[:], in_=ident_f128[:],
        compare_op=ALU.not_equal, fill=1.0, base=0, pattern=[[-1, 128]], channel_multiplier=1))
    raw("gpsimd", lambda e: e.memset(onesrow[:], 1.0))
    gp_const = step("gpsimd", "gp", lambda e: e.memset(eps_c[:], 1e-5))

    bu_r = bu_d.rearrange("(a p) -> a p", p=128)
    bv_r = bv_d.rearrange("(a p) -> a p", p=128)
    bg2_r = bg2_d.rearrange("(a p) -> a p", p=128)
    cb_r = convb_d.rearrange("(a p) -> a p", p=128)
    bg1_r = bg1_d.rearrange("(a p) -> a p", p=128)
    for j in range(16):
        step("sync", "sx", lambda e, j=j: e.dma_start(out=small[:, BU + j:BU + j + 1], in_=bu_r[j:j + 1, :]))
        step("sync", "sx", lambda e, j=j: e.dma_start(out=small[:, BV + j:BV + j + 1], in_=bv_r[j:j + 1, :]))
        step("sync", "sx", lambda e, j=j: e.dma_start(out=small[:, BG2 + j:BG2 + j + 1], in_=bg2_r[j:j + 1, :]))
        step("sync", "sx", lambda e, j=j: e.dma_start(out=small[:, CB + j:CB + j + 1], in_=cb_r[j:j + 1, :]))
    for m in range(4):
        step("sync", "sx", lambda e, m=m: e.dma_start(out=small[:, BG1 + m:BG1 + m + 1], in_=bg1_r[m:m + 1, :]))
    for i in range(15):
        step("sync", "sx", lambda e, i=i: e.dma_start(out=convw_sb[:, i * 3:(i + 1) * 3], in_=convw_d[i * 128:(i + 1) * 128, :]))
    sx_small = step("sync", "sx", lambda e: e.dma_start(out=convw_sb[:, 45:48], in_=convw_d[15 * 128:, :]))
    for i in range(15):
        step("gpsimd", "gw", lambda e, i=i: e.dma_start(out=rtr_bf[:, i * 16:(i + 1) * 16], in_=wr_d[i * 128:(i + 1) * 128, :]))
    gw_rtr = step("gpsimd", "gw", lambda e: e.dma_start(out=rtr_bf[:, 240:256], in_=wr_d[15 * 128:, :]))

    try:
        wait("tensor", gp=gp_const)
        wg1T = [at(f"wg1T{m}", [128, 2048], BF16, RA + m * 4096) for m in range(4)]
        raw_pe_free = [0, 0]
        wg1_ready = [None] * 4
        for m in range(4):
            k = m % 2
            wait("gpsimd", pe=raw_pe_free[k])
            gw = step("gpsimd", "gw", lambda e, m=m, k=k: e.dma_start(out=raw_b[k][:], in_=Wg1_d[m * 128:(m + 1) * 128, :]))
            wait("tensor", gw=gw)
            wg1_ready[m] = transp_batch(raw_b[k], None, 16,
                                        batch_ap_fn=lambda b0, n8, m=m: wg1T[m][:, b0 * 128:(b0 + n8) * 128])
            raw_pe_free[k] = counts["pe"]
        xst_free = [None, None]
        xntok_free = None
        p0_ac = None
        bank01_free = [None, None]
        pb2l_free = None
        ltT_dv = None
        xn3 = xn_sb[:].rearrange("p (i t) -> p i t", t=XS)
        for cg in range(4):
            xn_ev = {}
            for tt in range(4 * cg, 4 * cg + 4):
                k = tt % 2
                wait("sync", **merge(xst_free[k]))
                sx = step("sync", "sx", lambda e, tt=tt, k=k: e.dma_start(out=xstage[k][:], in_=x_d[tt * 128:(tt + 1) * 128, :]))
                wait("scalar", sx=sx, gp=gp_const, ac=counts["ac"], dv=counts["dv"])
                ac_s = step("scalar", "ac", lambda e, k=k: e.activation(cen[:], xstage[k][:], AF.Identity, accum_out=st1[:]))
                wait("vector", ac=ac_s)
                step("vector", "dv", lambda e: e.tensor_scalar(st2[:], st1[:], 1.0 / D, None, ALU.mult, op1=ALU.bypass))
                wait("vector", dv=counts["dv"])
                dv_c = step("vector", "dv", lambda e, k=k: e.tensor_scalar(cen[:], xstage[k][:], st2[:, 0:1], None, ALU.subtract, op1=ALU.bypass))
                wait("scalar", dv=dv_c)
                ac_q = step("scalar", "ac", lambda e, k=k: e.activation(xstage[k][:], cen[:], AF.Square, accum_out=st1[:]))
                xst_free[k] = {"ac": ac_q}
                wait("scalar", ac=ac_q)
                ac0 = step("scalar", "ac", lambda e: e.activation(st2[:], st1[:], AF.Ln, bias=eps_c[:, 0:1], scale=1.0 / D))
                wait("scalar", ac=ac0)
                ac = step("scalar", "ac", lambda e: e.activation(st3[:], st2[:], AF.Exp, scale=-0.5))
                p0_ac = ac
                wait("vector", ac=ac, pe=xntok_free)
                dv_xn = step("vector", "dv", lambda e: e.tensor_scalar(xntok[:], cen[:], st3[:, 0:1], None, ALU.mult, op1=ALU.bypass))
                wait("tensor", dv=dv_xn)
                xn_ev[tt] = transp_batch(xntok, None, 16,
                             batch_ap_fn=lambda b0, n8, tt=tt, xn3=xn3: xn3[:, b0:b0 + n8, 1 + tt * 128:1 + (tt + 1) * 128])
                xntok_free = counts["pe"]
            xnready = merge(*[xn_ev[t_] for t_ in xn_ev], *wg1_ready)
            for m in range(4):
                bank = pb[m % 2]
                wait("tensor", **merge(xnready, bank01_free[m % 2]))
                pe = None
                for i in range(TD):
                    pe = step("tensor", "pe", lambda e, bank=bank, i=i, cg=cg, m=m: e.matmul(
                        bank[:], wg1T[m][:, i * 128:(i + 1) * 128],
                        xn_sb[:, i * XS + 1 + cg * 512: i * XS + 1 + (cg + 1) * 512],
                        start=(i == 0), stop=(i == TD - 1)))
                wait("scalar", pe=pe, sx=sx_small)
                ac = step("scalar", "ac", lambda e, bank=bank, m=m, cg=cg: e.activation(
                    h_sb[:, m * 2048 + cg * 512: m * 2048 + (cg + 1) * 512], bank[:],
                    AF.Relu, bias=small[:, BG1 + m:BG1 + m + 1], scale=1.0))
                bank01_free[m % 2] = {"ac": ac}
            wait("tensor", gw=gw_rtr, **merge(pb2l_free))
            pe = None
            for i in range(TD):
                pe = step("tensor", "pe", lambda e, i=i, cg=cg: e.matmul(
                    pb[2][0:16, :], rtr_bf[:, i * 16:(i + 1) * 16],
                    xn_sb[:, i * XS + 1 + cg * 512: i * XS + 1 + (cg + 1) * 512],
                    start=(i == 0), stop=(i == TD - 1)))
            wait("vector", pe=pe)
            ltT_dv = step("vector", "dv", lambda e, cg=cg: e.tensor_copy(
                ltT[:, cg * 512:(cg + 1) * 512], pb[2][0:16, :]))
            pb2l_free = {"dv": ltT_dv}
        p0_done = snap()
        p1_done = p0_done

        if PHASES < 3: raise _Done()
        pb2_free = merge(pb2l_free)
        pT0_cw_free = None
        for tt in range(NTT):
            wait("tensor", **merge({"dv": ltT_dv}, pb2_free))
            pe = step("tensor", "pe", lambda e, tt=tt: e.transpose(
                pb[2][0:128, 0:16], ltT[:, tt * 128:(tt + 1) * 128], ident_f128[0:16, 0:16]))
            wait("vector", pe=pe)
            step("vector", "dv", lambda e: e.tensor_copy(lg[:], pb[2][0:128, 0:16]))
            pb2_free = {"dv": counts["dv"]}
            wait("vector", dv=counts["dv"])
            step("vector", "dv", lambda e: e.tensor_reduce(st1[:], lg[:], mybir.AxisListType.X, ALU.max))
            wait("vector", dv=counts["dv"])
            dv = step("vector", "dv", lambda e: e.tensor_scalar(st2[:], st1[:], -1.0, None, ALU.mult, op1=ALU.bypass))
            wait("scalar", dv=dv)
            ac = step("scalar", "ac", lambda e: e.activation(pp[:], lg[:], AF.Exp, bias=st2[:, 0:1], scale=1.0))
            wait("vector", ac=ac)
            step("vector", "dv", lambda e: e.tensor_copy(rr[:], pp[:]))
            for _ in range(3):
                wait("vector", dv=counts["dv"])
                step("vector", "dv", lambda e: e.tensor_reduce(st1[:], rr[:], mybir.AxisListType.X, ALU.max))
                wait("vector", dv=counts["dv"])
                step("vector", "dv", lambda e: e.scalar_tensor_tensor(rr[:], rr[:], st1[:, 0:1], rr[:], ALU.is_lt, ALU.mult))
            wait("vector", dv=counts["dv"])
            step("vector", "dv", lambda e: e.tensor_reduce(st1[:], rr[:], mybir.AxisListType.X, ALU.max))
            wait("vector", dv=counts["dv"])
            step("vector", "dv", lambda e: e.scalar_tensor_tensor(sel[:], pp[:], st1[:, 0:1], pp[:], ALU.is_ge, ALU.mult))
            wait("vector", dv=counts["dv"])
            step("vector", "dv", lambda e: e.tensor_reduce(st1[:], sel[:], mybir.AxisListType.X, ALU.add))
            wait("vector", dv=counts["dv"])
            step("vector", "dv", lambda e: e.reciprocal(st2[:], st1[:]))
            wait("vector", dv=counts["dv"])
            step("vector", "dv", lambda e: e.tensor_scalar(st3[:], st2[:], 0.8, None, ALU.mult, op1=ALU.bypass))
            wait("vector", dv=counts["dv"])
            dv = step("vector", "dv", lambda e: e.tensor_scalar(cwt_t[:], sel[:], st3[:, 0:1], None, ALU.mult, op1=ALU.bypass))
            wait("tensor", **merge({"dv": dv}, pT0_cw_free))
            pe = step("tensor", "pe", lambda e: e.transpose(pT[0][0:16, 0:128], cwt_t[:], ident_bf[:]))
            wait("vector", pe=pe)
            dv = step("vector", "dv", lambda e, tt=tt: e.tensor_copy(
                cwT_sb[0:16, tt * 128:(tt + 1) * 128], pT[0][0:16, 0:128]))
            pT0_cw_free = {"dv": dv}
        pT_free[0] = merge(pT_free[0], pT0_cw_free)
        p2_done = snap()
        wait("gpsimd", pe=p2_done["pe"], dv=p2_done["dv"], ac=p2_done["ac"])
        gp_cw02 = step("gpsimd", "gp", lambda e: e.memset(cw02[:], 0.2))
        if DEBUG:
            wait("sync", dv=counts["dv"], ac=counts["ac"])
            step("sync", "sx", lambda e: e.dma_start(out=dbg["d_xn"], in_=xn_sb[:]))
            step("sync", "sx", lambda e: e.dma_start(out=dbg["d_h"], in_=h_sb[:]))
            step("sync", "sx", lambda e: e.dma_start(out=dbg["d_cw"], in_=cwT_sb[:]))


        wait("gpsimd", dv=p0_done["dv"])
        cb_gw = step("gpsimd", "gw", lambda e: e.dma_start(
            out=cbrow[0:1, :], in_=convb_d.rearrange("(a b) -> a b", a=1)))
        diag_dv = [None, None]
        diag_pe_free = [None, None]
        pbc_free = [None, None, None]
        cst_free = None
        conv_dma = None
        for i in range(TD):
            ib = i % 2
            wait("vector", **merge(diag_pe_free[ib], {"dv": p0_done["dv"]}))
            for k3 in range(3):
                dv = step("vector", "dv", lambda e, ib=ib, k3=k3, i=i: e.tensor_scalar(
                    diagb[ib * 3 + k3][:], ident_bf[:], convw_sb[:, i * 3 + k3:i * 3 + k3 + 1],
                    None, ALU.mult, op1=ALU.bypass))
            diag_dv[ib] = dv
            wait("tensor", dv=diag_dv[ib], gw=cb_gw)
            for g4 in range(4):
                bk = (i * 4 + g4) % 3
                bank = pb[3 + bk]
                wait("tensor", **merge(pbc_free[bk]))
                pe = None
                for s in range(4):
                    tsub = g4 * 4 + s
                    base = i * XS + 1 + tsub * 128
                    step("tensor", "pe", lambda e, bank=bank, s=s, base=base, ib=ib: e.matmul(
                        bank[:, s * 128:(s + 1) * 128], xn_sb[:, base - 1:base + 127],
                        diagb[ib * 3][:], start=True, stop=False))
                    step("tensor", "pe", lambda e, bank=bank, s=s, base=base, ib=ib: e.matmul(
                        bank[:, s * 128:(s + 1) * 128], xn_sb[:, base:base + 128],
                        diagb[ib * 3 + 1][:], start=False, stop=False))
                    step("tensor", "pe", lambda e, bank=bank, s=s, base=base, ib=ib: e.matmul(
                        bank[:, s * 128:(s + 1) * 128], xn_sb[:, base + 1:base + 129],
                        diagb[ib * 3 + 2][:], start=False, stop=False))
                    pe = step("tensor", "pe", lambda e, bank=bank, s=s, i=i: e.matmul(
                        bank[:, s * 128:(s + 1) * 128], onesrow[:],
                        cbrow[0:1, i * 128:(i + 1) * 128], start=False, stop=True))
                w = {"pe": pe}
                if g4 == 0:
                    w = merge(w, {"sx": cst_free})
                wait("vector", **w)
                dv = step("vector", "dv", lambda e, bank=bank, g4=g4: e.tensor_copy(
                    cstage[:, g4 * 512:(g4 + 1) * 512], bank[:]))
                pbc_free[bk] = {"dv": dv}
            diag_pe_free[ib] = {"pe": counts["pe"]}
            wait("sync", dv=counts["dv"])
            conv_dma = step("sync", "sx", lambda e, i=i: e.dma_start(
                out=convT_scr.rearrange("(q p) (i f) -> p q i f", p=128, f=128)[:, :, i:i + 1, :],
                in_=cstage[:]))
            cst_free = conv_dma
        conv_done_sx = counts["sx"]
        conv_done_pe = counts["pe"]
        if PHASES < 4: raise _Done()
        st = {
            "wuT_ready": [None, None], "wvT_ready": [None, None], "wg2T_ready": [None, None],
            "wuT_free": [p1_done["pe"], p1_done["pe"]],
            "wvT_free": [0, 0], "wg2T_free": [0, 0],
            "scrw": [None, None],
            "rawu_pe": max(raw_pe_free), "rawv_pe": max(raw_pe_free), "wg2raw_pe": [0, 0],
        }
        wuT_scr_w = wuT_scr.rearrange("(b p) (a f) -> a p b f", p=128, f=128)

        def load_and_transpose_j(j):
            k = j % 2
            wait("gpsimd", pe=st["rawu_pe"])
            gw = step("gpsimd", "gw", lambda e, j=j: e.dma_start(out=raw_b[0][:], in_=Wu_d[j * 128:(j + 1) * 128, :]))
            wait("tensor", gw=gw)
            pre = merge({"pe": st["wuT_free"][k]}, st["scrw"][k])
            ev = transp_batch(raw_b[0], None, 16, pre_evac=pre,
                              batch_ap_fn=lambda b0, n8, k=k: wuT_b[k][:, b0 * 128:(b0 + n8) * 128])
            st["rawu_pe"] = counts["pe"]
            st["wuT_ready"][k] = ev
            wait("sync", **ev)
            st["scrw"][k] = {"sx": step("sync", "sx", lambda e, k=k, j=j: e.dma_start(
                out=wuT_scr_w[j:j + 1], in_=wuT_b[k][:]))}
            wait("gpsimd", pe=st["rawv_pe"])
            gw = step("gpsimd", "gw", lambda e, j=j: e.dma_start(out=raw_b[1][:], in_=Wv_d[j * 128:(j + 1) * 128, :]))
            wait("tensor", gw=gw)
            ev = transp_batch(raw_b[1], None, 16, pre_evac={"pe": st["wvT_free"][k]},
                              batch_ap_fn=lambda b0, n8, k=k: wvT_b[k][:, b0 * 128:(b0 + n8) * 128])
            st["rawv_pe"] = counts["pe"]
            st["wvT_ready"][k] = ev
            wait("gpsimd", pe=st["wg2raw_pe"][k])
            gw = step("gpsimd", "gw", lambda e, j=j, k=k: e.dma_start(out=wg2raw_b[k][:], in_=Wg2_d[j * 128:(j + 1) * 128, :]))
            wait("tensor", gw=gw)
            ev = transp_batch(wg2raw_b[k], None, 4, pre_evac={"pe": st["wg2T_free"][k]},
                              batch_ap_fn=lambda b0, n8, k=k: wg2T_b[k][:, b0 * 128:(b0 + n8) * 128])
            st["wg2raw_pe"][k] = counts["pe"]
            st["wg2T_ready"][k] = ev

        ubank_free = [merge(bank01_free[0]), merge(bank01_free[1])]
        vbank_free = [merge(pb2_free), merge(pbc_free[0])]
        gbank_free = [merge(pbc_free[1]), merge(pbc_free[2])]
        gates_rd = None
        wait("vector", pe=conv_done_pe)
        wait("scalar", sx=conv_done_sx)
        load_and_transpose_j(0)
        for j in range(NJ):
            k = j % 2
            wait("tensor", **merge(st["wuT_ready"][k], st["wvT_ready"][k], st["wg2T_ready"][k]))
            for c in range(4):
                gb, ub, vb = pb[4 + (c % 2)], pb[c % 2], pb[2 + (c % 2)]
                wait("tensor", **merge(gbank_free[c % 2]))
                pe = None
                for m in range(4):
                    pe = step("tensor", "pe", lambda e, gb=gb, m=m, c=c, k=k: e.matmul(
                        gb[:], wg2T_b[k][:, m * 128:(m + 1) * 128],
                        h_sb[:, m * 2048 + c * 512: m * 2048 + (c + 1) * 512],
                        start=(m == 0), stop=(m == 3)))
                wait("scalar", pe=pe, dv=gates_rd)
                ac_g = step("scalar", "ac", lambda e, gb=gb, j=j, c=c: e.activation(
                    gates[:, c * 512:(c + 1) * 512], gb[:], AF.Sigmoid,
                    bias=small[:, BG2 + j:BG2 + j + 1], scale=1.0))
                gbank_free[c % 2] = {"ac": ac_g}
                wait("tensor", **merge(ubank_free[c % 2]))
                pe = None
                for i in range(TD):
                    pe = step("tensor", "pe", lambda e, ub=ub, i=i, c=c, k=k: e.matmul(
                        ub[:], wuT_b[k][:, i * 128:(i + 1) * 128],
                        xn_sb[:, i * XS + 1 + c * 512: i * XS + 1 + (c + 1) * 512],
                        start=(i == 0), stop=(i == TD - 1)))
                wait("vector", pe=pe, ac=ac_g)
                dv_u = step("vector", "dv", lambda e, ub=ub, j=j, c=c: e.scalar_tensor_tensor(
                    u_g[:, c * 512:(c + 1) * 512], ub[:], small[:, BU + j:BU + j + 1],
                    gates[:, c * 512:(c + 1) * 512], ALU.add, ALU.mult))
                ubank_free[c % 2] = {"dv": dv_u}
                wait("tensor", **merge(vbank_free[c % 2]))
                pe = None
                for i in range(TD):
                    pe = step("tensor", "pe", lambda e, vb=vb, i=i, c=c, k=k: e.matmul(
                        vb[:], wvT_b[k][:, i * 128:(i + 1) * 128],
                        xn_sb[:, i * XS + 1 + c * 512: i * XS + 1 + (c + 1) * 512],
                        start=(i == 0), stop=(i == TD - 1)))
                wait("vector", pe=pe)
                dv_v = step("vector", "dv", lambda e, vb=vb, j=j, c=c: e.scalar_tensor_tensor(
                    v_g[:, c * 512:(c + 1) * 512], vb[:], small[:, BV + j:BV + j + 1],
                    gates[:, c * 512:(c + 1) * 512], ALU.add, ALU.mult))
                vbank_free[c % 2] = {"dv": dv_v}
            st["wuT_free"][k] = counts["pe"]
            st["wvT_free"][k] = counts["pe"]
            st["wg2T_free"][k] = counts["pe"]
            if j + 1 < NJ:
                load_and_transpose_j(j + 1)
            wait("vector", dv=counts["dv"])
            step("vector", "dv", lambda e: e.tensor_tensor_scan(
                cum[:], u_g[:], u_g[:], 0.0, ALU.add, ALU.bypass))
            wait("vector", dv=counts["dv"])
            dv_go = step("vector", "dv", lambda e, j=j: e.tensor_tensor(
                go_sb[:, j * 2048:(j + 1) * 2048], cum[:], v_g[:], ALU.mult))
            gates_rd = dv_go
            if DEBUG and j == 0:
                wait("sync", dv=dv_go)
                step("sync", "sx", lambda e: e.dma_start(out=dbg["d_ug"], in_=u_g[:]))
                step("sync", "sx", lambda e: e.dma_start(out=dbg["d_cum"], in_=cum[:]))
        p3_done = snap()
        if DEBUG:
            wait("sync", dv=counts["dv"])
            step("sync", "sx", lambda e: e.dma_start(out=dbg["d_go"], in_=go_sb[:]))

        if PHASES < 5: raise _Done()
        for engn in ("sync", "gpsimd", "scalar"):
            wait(engn, pe=p3_done["pe"], dv=p3_done["dv"], ac=p3_done["ac"], sx=p3_done["sx"])
        wuT_scr_rc = wuT_scr.rearrange("(b p) d -> p b d", p=128)
        p4_ready = [None, None]
        p4_free = [None, None]
        p4x_ready = [None] * 4
        p4x_free = [None] * 4
        pb_free4 = [merge(ubank_free[0]), merge(ubank_free[1]), merge(vbank_free[0]),
                    merge(vbank_free[1]), merge(gbank_free[0]), merge(gbank_free[1])]
        BMAP = [[0, 1, 2, 3], [4, 5, 0, 1]]

        def p4_load(idx):
            kk = idx % 2
            pr, rem = divmod(idx, 4)
            dch, piece = divmod(rem, 2)
            engn = "sync" if kk == 0 else "gpsimd"
            semk = "sx" if engn == "sync" else "gw"
            wait(engn, **merge(p4_free[kk]))
            p4_ready[kk] = {semk: step(engn, semk, lambda e, piece=piece, dch=dch, kk=kk: e.dma_start(
                out=p4wuT[kk][:],
                in_=wuT_scr_rc[:, piece * 8:(piece + 1) * 8, dch * 1024:(dch + 1) * 1024]))}

        def p4_loadx(tt):
            sl = tt % 4
            wait("scalar", **merge(p4x_free[sl]))
            p4x_ready[sl] = step("scalar", "ax", lambda e, tt=tt, sl=sl: e.dma_start(
                out=p4ax[sl][:], in_=x_d[tt * 128:(tt + 1) * 128, :]))

        p4_load(0); p4_load(1)
        p4_loadx(0); p4_loadx(1); p4_loadx(2); p4_loadx(3)
        for pr in range(8):
            ax_dv = [None, None]
            for dch in range(2):
                for piece in range(2):
                    idx = pr * 4 + dch * 2 + piece
                    kk = idx % 2
                    wait("tensor", **merge(p4_ready[kk]))
                    for jo in range(8):
                        j = piece * 8 + jo
                        for tq in range(2):
                            tt = 2 * pr + tq
                            for dq in range(2):
                                bank = pb[BMAP[dch][tq * 2 + dq]]
                                if j == 0:
                                    wait("tensor", **merge(pb_free4[BMAP[dch][tq * 2 + dq]]))
                                step("tensor", "pe", lambda e, j=j, jo=jo, dq=dq, tt=tt, kk=kk, bank=bank: e.matmul(
                                    bank[:], go_sb[:, j * 2048 + tt * 128: j * 2048 + (tt + 1) * 128],
                                    p4wuT[kk][:, jo * 1024 + dq * 512: jo * 1024 + (dq + 1) * 512],
                                    start=(j == 0), stop=(j == NJ - 1)))
                    p4_free[kk] = {"pe": counts["pe"]}
                    if idx + 2 < 32:
                        p4_load(idx + 2)
                pe_done = counts["pe"]
                for tq in range(2):
                    tt = 2 * pr + tq
                    sl = tt % 4
                    wait("vector", pe=pe_done, ax=p4x_ready[sl])
                    for dq in range(2):
                        bk = BMAP[dch][tq * 2 + dq]
                        dv = step("vector", "dv", lambda e, sl=sl, dch=dch, dq=dq, bk=bk: e.tensor_tensor(
                            p4ax[sl][:, dch * 1024 + dq * 512: dch * 1024 + (dq + 1) * 512],
                            p4ax[sl][:, dch * 1024 + dq * 512: dch * 1024 + (dq + 1) * 512],
                            pb[bk][:], ALU.add))
                        pb_free4[bk] = {"dv": dv}
                    ax_dv[tq] = dv if dch == 1 else None
            for tq in range(2):
                tt = 2 * pr + tq
                sl = tt % 4
                wait("sync", dv=counts["dv"])
                sxs = step("sync", "sx", lambda e, tt=tt, sl=sl: e.dma_start(
                    out=attnx_scr[tt * 128:(tt + 1) * 128, :], in_=p4ax[sl][:]))
                p4x_free[sl] = {"sx": sxs}
            if 2 * pr + 4 < NTT:
                p4_loadx(2 * pr + 4)
            if 2 * pr + 5 < NTT:
                p4_loadx(2 * pr + 5)
        p4_done = snap()
        if DEBUG:
            wait("sync", sx=counts["sx"])
            step("sync", "sx", lambda e: e.dma_start(out=dbg["d_ax"], in_=attnx_scr[:, :]))

        if PHASES < 6: raise _Done()
        for engn in ENGS:
            wait(engn, **{k: v for k, v in p4_done.items() if v})

        NU = E + 4
        out_store = [[None] * 8, [None] * 8]
        we1_ready = [None, None]; we1_free = [None, None]
        we2_ready = [None, None]; we2_free = [None, None]
        cwb_ready = [None, None]
        cwrow_pe_free = None
        gB = [None, None, None, None]
        wB = [None, None]
        acc_wr = None

        we1_r = we1_d.rearrange("(e n p) h -> e p n h", n=TD, p=128)
        we3_r = we3_d.rearrange("(e n p) h -> e p n h", n=TD, p=128)
        we2_r = we2_d.rearrange("(e n p) d -> e p n d", n=8, p=128)
        ws1_r = ws1_d.rearrange("(n p) h -> p n h", p=128)
        ws3_r = ws3_d.rearrange("(n p) h -> p n h", p=128)
        ws2_r = ws2_d.rearrange("(m n p) d -> m p n d", m=4, p=128)

        def w1_ap(u, q, which):
            if u < E:
                rr_ = we1_r if which == "w1" else we3_r
                return rr_[u:u + 1, :, :, q * 256:(q + 1) * 256]
            qq = u - E
            rr_ = ws1_r if which == "w1" else ws3_r
            return rr_[:, :, qq * 1024 + q * 256: qq * 1024 + (q + 1) * 256]

        def w2_ap(u, dc):
            if u < E:
                return we2_r[u:u + 1, :, :, dc * 512:(dc + 1) * 512]
            return ws2_r[u - E:u - E + 1, :, :, dc * 512:(dc + 1) * 512]

        acc_tb_done = [[None] * 8, [None] * 8]

        def emit_C(half):
            sx0 = counts["sx"]
            for tb in range(8):
                tt = half * 8 + tb
                wait("gpsimd", **merge(acc_tb_done[half][tb], {"sx": sx0}))
                step("gpsimd", "gw", lambda e, tb=tb, tt=tt: e.dma_start(
                    out=acc[:, tb * 2048:(tb + 1) * 2048],
                    in_=attnx_scr[tt * 128:(tt + 1) * 128, :], accum_op=ALU.add))
                g2 = step("gpsimd", "gw", lambda e, tb=tb, tt=tt: e.dma_start(
                    out=acc[:, tb * 2048:(tb + 1) * 2048],
                    in_=convT_scr[tt * 128:(tt + 1) * 128, :], accum_op=ALU.add))
                wait("sync", gw=g2)
                out_store[half][tb] = step("sync", "sx", lambda e, tb=tb, tt=tt: e.dma_start(
                    out=out_d[tt * 128:(tt + 1) * 128, :], in_=acc[:, tb * 2048:(tb + 1) * 2048]))
                yield

        def emit_half(half, pump=None):
            nonlocal cwrow_pe_free, acc_wr

            late_gen = [None]

            def do_pump():
                if pump is not None:
                    next(pump, None)
            w1seq = [(u, q, w) for u in range(NU) for q in range(4) for w in ("w1", "w3")]
            w2seq = [(u, dc) for u in range(NU) for dc in range(4)]

            def load_w1(idx):
                kk = idx % 2
                u, q, which = w1seq[idx]
                wait("gpsimd", **merge(we1_free[kk]))
                we1_ready[kk] = {"gw": step("gpsimd", "gw", lambda e, u=u, q=q, which=which, kk=kk: e.dma_start(
                    out=we1buf[kk][:], in_=w1_ap(u, q, which)))}

            def load_w2(idx):
                kk = idx % 2
                u, dc = w2seq[idx]
                wait("gpsimd", **merge(we2_free[kk]))
                we2_ready[kk] = {"gw": step("gpsimd", "gw", lambda e, u=u, dc=dc, kk=kk: e.dma_start(
                    out=we2buf[kk][:], in_=w2_ap(u, dc)))}

            load_w1(0); load_w1(1); load_w2(0); load_w2(1)
            cw_sx = [None, None]

            def load_cwrow(u):
                sl = u % 2
                wait("sync", pe=cwrow_pe_free)
                cw_sx[sl] = step("sync", "sx", lambda e, u=u, sl=sl, half=half: e.dma_start(
                    out=cwrow[0:1, sl * 1024:(sl + 1) * 1024],
                    in_=cwT_sb[u:u + 1, half * 1024:(half + 1) * 1024]))

            load_cwrow(0)
            for u in range(NU):
                ge_hb = [None] * 8
                kk_cw = u % 2
                if u + 1 < E:
                    load_cwrow(u + 1)
                if u < E:
                    sl = u % 2
                    wait("tensor", sx=cw_sx[sl])
                    for cc in range(2):
                        wait("tensor", **merge(gB[cc]))
                    pe = None
                    for cc in range(2):
                        pe = step("tensor", "pe", lambda e, cc=cc, sl=sl: e.matmul(
                            pb[cc][:], onesrow[:], cwrow[0:1, sl * 1024 + cc * 512: sl * 1024 + (cc + 1) * 512],
                            start=True, stop=True))
                    cwrow_pe_free = pe
                    dv = None
                    wait("vector", pe=pe)
                    for cc in range(2):
                        dv = step("vector", "dv", lambda e, cc=cc, kk_cw=kk_cw: e.tensor_copy(
                            cwbb[kk_cw][:, cc * 512:(cc + 1) * 512], pb[cc][:]))
                        gB[cc] = {"dv": dv}
                    cwb_ready[kk_cw] = {"dv": dv}
                for q in range(4):
                    idx = (u * 4 + q) * 2
                    kk1, kk3 = idx % 2, (idx + 1) % 2
                    wait("tensor", **merge(we1_ready[kk1]))
                    for c in range(2):
                        for mp in range(2):
                            bank = pb[mp * 2 + c]
                            wait("tensor", **merge(gB[mp * 2 + c]))
                            pe = None
                            for i in range(TD):
                                pe = step("tensor", "pe", lambda e, bank=bank, i=i, mp=mp, c=c, kk1=kk1, half=half: e.matmul(
                                    bank[:], we1buf[kk1][:, i * 256 + mp * 128: i * 256 + (mp + 1) * 128],
                                    xn_sb[:, i * XS + 1 + half * 1024 + c * 512: i * XS + 1 + half * 1024 + (c + 1) * 512],
                                    start=(i == 0), stop=(i == TD - 1)))
                            wait("scalar", pe=pe)
                            ac = step("scalar", "ac", lambda e, bank=bank, mp=mp, c=c: e.activation(
                                g1pair[:, mp * 1024 + c * 512: mp * 1024 + (c + 1) * 512], bank[:], AF.Silu))
                            gB[mp * 2 + c] = {"ac": ac}
                    we1_free[kk1] = {"pe": counts["pe"]}
                    if idx + 2 < len(w1seq):
                        load_w1(idx + 2)
                    do_pump()
                    wait("tensor", **merge(we1_ready[kk3]))
                    for c in range(2):
                        for mp in range(2):
                            bank = pb[mp * 2 + c]
                            wait("tensor", **merge(gB[mp * 2 + c]))
                            pe = None
                            for i in range(TD):
                                pe = step("tensor", "pe", lambda e, bank=bank, i=i, mp=mp, c=c, kk3=kk3, half=half: e.matmul(
                                    bank[:], we1buf[kk3][:, i * 256 + mp * 128: i * 256 + (mp + 1) * 128],
                                    xn_sb[:, i * XS + 1 + half * 1024 + c * 512: i * XS + 1 + half * 1024 + (c + 1) * 512],
                                    start=(i == 0), stop=(i == TD - 1)))
                            cwbuf = cw02 if u >= E else cwbb[kk_cw]
                            w = {"pe": pe}
                            if u < E:
                                w = merge(w, cwb_ready[kk_cw])
                            else:
                                w = merge(w, {"gp": gp_cw02})
                            wait("vector", **w)
                            step("vector", "dv", lambda e, bank=bank, c=c, cwbuf=cwbuf, half=half: e.tensor_tensor(
                                tmp512[:], bank[:], cwbuf[:, c * 512:(c + 1) * 512], ALU.mult))
                            dv = step("vector", "dv", lambda e, q=q, mp=mp, c=c: e.tensor_tensor(
                                g_e[:, (q * 2 + mp) * 1024 + c * 512: (q * 2 + mp) * 1024 + (c + 1) * 512],
                                tmp512[:], g1pair[:, mp * 1024 + c * 512: mp * 1024 + (c + 1) * 512], ALU.mult))
                            gB[mp * 2 + c] = {"dv": dv}
                            ge_hb[q * 2 + mp] = dv
                    we1_free[kk3] = {"pe": counts["pe"]}
                    if idx + 3 < len(w1seq):
                        load_w1(idx + 3)
                    do_pump()
                ge_dv = counts["dv"]
                for dc in range(4):
                    idx2 = u * 4 + dc
                    kk2 = idx2 % 2
                    if u == NU - 1 and dc == 3 and late_gen[0] is None:
                        late_gen[0] = emit_C(half)
                    wait("tensor", **merge(we2_ready[kk2]))
                    for tb in range(8):
                        bank = pb[4 + (tb % 2)]
                        wait("tensor", **merge(wB[tb % 2]))
                        pe = None
                        for hb in range(8):
                            if dc == 0 and tb == 0:
                                wait("tensor", dv=ge_hb[hb])
                            pe = step("tensor", "pe", lambda e, bank=bank, hb=hb, tb=tb, kk2=kk2: e.matmul(
                                bank[:], g_e[:, hb * 1024 + tb * 128: hb * 1024 + (tb + 1) * 128],
                                we2buf[kk2][:, hb * 512:(hb + 1) * 512],
                                start=(hb == 0), stop=(hb == 7)))
                        w2w = {"pe": pe}
                        if u == 0 and half == 1:
                            w2w = merge(w2w, {"sx": out_store[0][tb]})
                        wait("vector", **w2w)
                        if u == 0:
                            dv = step("vector", "dv", lambda e, bank=bank, tb=tb, dc=dc: e.tensor_copy(
                                acc[:, tb * 2048 + dc * 512: tb * 2048 + (dc + 1) * 512], bank[:]))
                        else:
                            dv = step("vector", "dv", lambda e, bank=bank, tb=tb, dc=dc: e.tensor_tensor(
                                acc[:, tb * 2048 + dc * 512: tb * 2048 + (dc + 1) * 512],
                                acc[:, tb * 2048 + dc * 512: tb * 2048 + (dc + 1) * 512], bank[:], ALU.add))
                        wB[tb % 2] = {"dv": dv}
                        if u == NU - 1 and dc == 3:
                            acc_tb_done[half][tb] = {"dv": dv}
                            if late_gen[0] is not None and tb < 2:
                                next(late_gen[0], None)
                    we2_free[kk2] = {"pe": counts["pe"]}
                    if idx2 + 2 < len(w2seq):
                        load_w2(idx2 + 2)
            acc_wr = counts["dv"]
            if DEBUG and half == 0:
                wait("sync", dv=acc_wr)
                step("sync", "sx", lambda e: e.dma_start(out=dbg["d_acc"], in_=acc[:]))

            return late_gen[0]

        genC0 = emit_half(0)
        if PHASES < 7:
            for _ in genC0:
                pass
            raise _Done()
        h0 = snap()
        for engn in ("tensor", "vector", "scalar"):
            wait(engn, pe=h0["pe"], dv=h0["dv"], ac=h0["ac"])
        genC1 = emit_half(1, pump=genC0)
        for _ in genC0:
            pass
        for _ in genC1:
            pass

    except _Done:
        pass
    fin = dict(counts)

    def _finwait(eng):
        if fin["sx"]:
            eng.wait_ge(semSX, 16 * fin["sx"])
        if fin["gw"]:
            eng.wait_ge(semGW, 16 * fin["gw"])
        if fin.get("ax"):
            eng.wait_ge(semAX, 16 * fin["ax"])
    streams["sync"].append(_finwait)

    _replay(block, streams)
    stk.close()
    return nc


_NC_CACHE = None


def kernel(L0_attn_x, L0_attn_Wu, L0_attn_bu, L0_attn_Wv, L0_attn_bv,
           L0_attn_Wg1, L0_attn_bg1, L0_attn_Wg2, L0_attn_bg2,
           L0_attn_conv_w, L0_attn_conv_b,
           L0_ffn_w_router, L0_ffn_we1, L0_ffn_we3, L0_ffn_we2,
           L0_ffn_ws1, L0_ffn_ws3, L0_ffn_ws2):
    global _NC_CACHE
    if _NC_CACHE is None:
        _NC_CACHE = build_nc()
    nc = _NC_CACHE
    f32 = np.float32
    base = {
        "Wu": np.ascontiguousarray(L0_attn_Wu, f32), "bu": np.ascontiguousarray(L0_attn_bu, f32),
        "Wv": np.ascontiguousarray(L0_attn_Wv, f32), "bv": np.ascontiguousarray(L0_attn_bv, f32),
        "Wg1": np.ascontiguousarray(L0_attn_Wg1, f32), "bg1": np.ascontiguousarray(L0_attn_bg1, f32),
        "Wg2": np.ascontiguousarray(L0_attn_Wg2, f32), "bg2": np.ascontiguousarray(L0_attn_bg2, f32),
        "conv_w": np.ascontiguousarray(np.asarray(L0_attn_conv_w, f32).reshape(D, 3)),
        "conv_b": np.ascontiguousarray(L0_attn_conv_b, f32),
        "w_router": np.ascontiguousarray(L0_ffn_w_router, f32),
        "we1": np.ascontiguousarray(np.asarray(L0_ffn_we1, f32).reshape(E * D, H)),
        "we3": np.ascontiguousarray(np.asarray(L0_ffn_we3, f32).reshape(E * D, H)),
        "we2": np.ascontiguousarray(np.asarray(L0_ffn_we2, f32).reshape(E * H, D)),
        "ws1": np.ascontiguousarray(L0_ffn_ws1, f32), "ws3": np.ascontiguousarray(L0_ffn_ws3, f32),
        "ws2": np.ascontiguousarray(L0_ffn_ws2, f32),
    }
    in_maps = []
    for b in range(8):
        m = dict(base)
        m["x"] = np.ascontiguousarray(L0_attn_x[b], f32)
        in_maps.append(m)
    res = run_bass_kernel_spmd(nc, in_maps, core_ids=list(range(8)))
    return np.stack([np.asarray(r["out"], dtype=np.float32) for r in res.results], axis=0)
